# version 08 (final)
# baseline (speedup 1.0000x reference)
import math
import os
import time
from contextlib import ExitStack
KSKIP = set(os.environ.get("KSKIP", "").split(","))
KDBG = os.environ.get("KDBG", "") != ""
KSEC = os.environ.get("KSEC", "")

import numpy as np
import ml_dtypes

import concourse.bass as bass
import concourse.mybir as mybir
from concourse.bass_utils import run_bass_kernel_spmd

F32 = mybir.dt.float32
BF16 = mybir.dt.bfloat16
AF = mybir.ActivationFunctionType
ALU = mybir.AluOpType
AX = mybir.AxisListType

BS = 512
NCORES = 8
SC_SCORE = 1.0 / math.sqrt(128.0)
EPS_LN = 1e-5
EPS_Q = 1e-8

SEMNAMES = ["DQS", "DQG", "DQG1", "KX0", "KX1", "P0", "P1", "X1R", "H2R",
            "FWD1", "FDD1", "GTU1", "GTD1",
            "GY", "G1", "GH2", "GTU", "GTD", "FWD", "FDD", "FQS",
            "VS", "SR", "VH", "TT", "VD", "VQ", "TS", "SE", "VM", "TO", "TL",
            "VOA", "VLA", "TY", "VYR", "VY", "VW", "SW", "SCD", "TSC", "V1",
            "S1", "TH2", "VH2", "FQ", "TUU", "TUG", "SS", "VHID", "TD",
            "VFT", "TF", "SF", "VF", "OUT", "VYC", "VX"]


class Sched:
    ENGINES = ("sync", "gpsimd", "scalar", "vector", "tensor")

    def __init__(self, base=None):
        self.ev = {e: [] for e in self.ENGINES}
        self.cnt = dict(base) if base else {n: 0 for n in SEMNAMES}
        self.lastwait = {}

    def on(self, engine, fn, waits=(), inc=None, inck=1):
        w = []
        for s, v in waits:
            if v <= 0:
                continue
            key = (engine, s)
            if self.lastwait.get(key, -1) < v:
                self.lastwait[key] = v
                w.append((s, v))
        self.ev[engine].append((fn, w, inc, inck))
        if inc is not None:
            self.cnt[inc] += inck
        return self.cnt.get(inc)

    def run(self, engine, eng, S):
        for fn, w, inc, inck in self.ev[engine]:
            for s, v in w:
                eng.wait_ge(S[s], v)
            inst = fn(eng)
            if inc is not None:
                inst.then_inc(S[inc], inck)


def build_program(H=16, NA=8, NB=16, DE=8192, stages=3):
    D = H * 128
    NT = 2 * BS
    NSLOT = NA + NB
    NEC = DE // 128
    NDS = D // 128
    DGS = [list(range(g, min(g + 4, NDS))) for g in range(0, NDS, 4)]

    nc = bass.Bass("TRN2", target_bir_lowering=False,
                   detect_race_conditions=False)

    xq_d = nc.dram_tensor("xq", [NT, D], F32, kind="ExternalInput").ap()
    ka_d = nc.dram_tensor("ka", [NA * BS, D], F32, kind="ExternalInput").ap()
    kb_d = nc.dram_tensor("kb", [NB * BS, D], F32, kind="ExternalInput").ap()
    wv_d = nc.dram_tensor("wv", [128, 4 * NSLOT], BF16, kind="ExternalInput").ap()
    m0_d = nc.dram_tensor("m0", [128, 1024], BF16, kind="ExternalInput").ap()
    m1_d = nc.dram_tensor("m1", [128, 1024], BF16, kind="ExternalInput").ap()
    idb_d = nc.dram_tensor("idb", [128, 128], BF16, kind="ExternalInput").ap()
    idf_d = nc.dram_tensor("idf", [128, 128], F32, kind="ExternalInput").ap()
    onef_d = nc.dram_tensor("onef", [128, 1], F32, kind="ExternalInput").ap()
    cstv_d = nc.dram_tensor("cstv", [128, 4], F32, kind="ExternalInput").ap()
    wup_d = nc.dram_tensor("wup", [NEC, 128, D], F32, kind="ExternalInput").ap()
    wgt_d = nc.dram_tensor("wgt", [NEC, 128, D], F32, kind="ExternalInput").ap()
    wdn_d = nc.dram_tensor("wdn", [DE, D], F32, kind="ExternalInput").ap()
    out_d = nc.dram_tensor("out", [NT, D], F32, kind="ExternalOutput").ap()
    nc.dram_tensor("pad", [1, int(time.time() * 10) % 9973 + 1], F32,
                   kind="ExternalOutput")
    if KDBG:
        PS_O = H * 512
        PS_L = max(1, H // 2) * 512
        dbgf_d = nc.dram_tensor("dbgf", [128, PS_O + PS_L + 192 + 16 * H], F32,
                                kind="ExternalOutput").ap()
        dbgb_d = nc.dram_tensor("dbgb", [128, 8 * D + H * 512 + 2048], BF16,
                                kind="ExternalOutput").ap()
        dbgq_d = nc.dram_tensor("dbgq", [128, 1024], F32,
                                kind="ExternalOutput").ap()
        dbgt_d = nc.dram_tensor("dbgt", [2, 128, 512], BF16,
                                kind="ExternalOutput").ap()

    ypark = nc.dram_tensor("ypark", [NT, D], F32, kind="Internal").ap()
    x1park = nc.dram_tensor("x1park", [NT, D], F32, kind="Internal").ap()
    h2park = nc.dram_tensor("h2park", [NT, D], F32, kind="Internal").ap()
    tqu_park = nc.dram_tensor("tqu_park", [NEC, 128, D], BF16, kind="Internal").ap()
    tqg_park = nc.dram_tensor("tqg_park", [NEC, 128, D], BF16, kind="Internal").ap()
    tqd_park = nc.dram_tensor("tqd_park", [len(DGS), NEC, 128, 512], BF16,
                              kind="Internal").ap()

    es = ExitStack()
    S = {n: es.enter_context(nc.semaphore(n)) for n in SEMNAMES}

    scb = es.enter_context(nc.sbuf_tensor("scb", [128, 16], F32))
    sc1 = es.enter_context(nc.sbuf_tensor("sc1", [1, 16], F32))
    idb = es.enter_context(nc.sbuf_tensor("idb_s", [128, 128], BF16))
    idf = es.enter_context(nc.sbuf_tensor("idf_s", [128, 128], F32))
    onef = es.enter_context(nc.sbuf_tensor("onef_s", [128, 1], F32))
    accw = es.enter_context(nc.sbuf_tensor("accw", [128, 3 * NEC * 4], F32))
    acc3 = es.enter_context(nc.sbuf_tensor("acc3", [128, 12], F32))
    cstv = es.enter_context(nc.sbuf_tensor("cstv_s", [128, 4], F32))
    nc.const_aps.register(F32, EPS_LN, cstv[:, 0:1])
    nc.const_aps.register(F32, EPS_Q, cstv[:, 1:2])

    b = Sched()

    with ExitStack() as ph1:
        def sb(name, shape, dt):
            return ph1.enter_context(nc.sbuf_tensor(name, shape, dt))

        xraw = [sb(f"xraw{i}", [128, 4 * D], F32) for i in range(2)]
        hT = [sb(f"hT{i}", [128, 4 * D], BF16) for i in range(2)]
        hDj = [sb(f"hDj{i}", [128, 512], BF16) for i in range(2)]
        hQ = sb("hQ", [128, H * 512], BF16)
        osb = sb("osb", [128, H * 512], F32)
        lsb = sb("lsb", [128, max(1, H // 2) * 512], F32)
        linv = sb("linv", [128, 4 * H], F32)
        linvr = sb("linvr", [128, 4 * H], F32)
        eb = [sb(f"eb{i}", [128, 1024], BF16) for i in range(2)]
        masks = [sb(f"mask{i}", [128, 1024], BF16) for i in range(2)]
        wvs = sb("wvs", [128, 4 * NSLOT], BF16)
        ysect = [sb("ysect0", [128, D], F32)]
        ysect = [ysect[0], ysect[0]]
        wp1 = [sb(f"wp1_{i}", [128, D], F32) for i in range(2)]
        sqscr = sb("sqscr", [128, D], BF16)
        stc = sb("stc", [128, 32], F32)

        tpsum = ph1.enter_context(nc.psum_tensor("tpsum", [128, 512], BF16))
        sps = [ph1.enter_context(nc.psum_tensor(f"sps{i}", [128, 1024], F32))
               for i in range(2)]
        ops = ph1.enter_context(nc.psum_tensor("ops", [128, 512], F32))
        lps = ph1.enter_context(nc.psum_tensor("lps", [128, 512], F32))
        yps = ph1.enter_context(nc.psum_tensor("yps", [128, 512], F32))

        def ld(dst, src):
            return lambda e: e.dma_start(out=dst, in_=src)

        for dst, src in ((idb[:, :], idb_d), (idf[:, :], idf_d),
                         (onef[:, :], onef_d), (wvs[:, :], wv_d),
                         (cstv[:, :], cstv_d),
                         (masks[0][:, :], m0_d), (masks[1][:, :], m1_d)):
            b.on("sync", ld(dst, src[:, :]), inc="DQS", inck=16)
        CONSTS = b.cnt["DQS"]

        p1_jobs = ([("u", i) for i in range(NEC)]
                   + [("g", i) for i in range(NEC)]
                   + [("d", i) for i in range(NEC)])
        p1_next = [0]
        p1_done = []

        def p1_emit(njobs):
            if "p1" in KSKIP:
                return
            for _ in range(njobs):
                idx = p1_next[0]
                if idx >= len(p1_jobs):
                    return
                kind, i = p1_jobs[idx]
                j = idx % 2
                p1_next[0] += 1
                if kind == "d":
                    srcap = wdn_d[i * 128:(i + 1) * 128, :]
                else:
                    srcap = (wup_d if kind == "u" else wgt_d)[i, :, :]
                col = {"u": 0, "g": NEC, "d": 2 * NEC}[kind] + i
                w = [p1_done[idx - 2]] if idx >= 2 else []
                dqp = "DQG" if j == 0 else "DQG1"
                b.on("gpsimd", ld(wp1[j][:, :], srcap), waits=w,
                     inc=dqp, inck=16)
                dq = (dqp, b.cnt[dqp])
                SW4 = D // 4
                for sc4 in range(4):
                    v = b.on("vector",
                             (lambda jj, cc, s4_: lambda e: e.reduce_sum(
                                 accw[:, 4 * cc + s4_:4 * cc + s4_ + 1],
                                 wp1[jj][:, s4_ * SW4:(s4_ + 1) * SW4], AX.X,
                                 apply_absolute_value=True))(j, col, sc4),
                             waits=[dq], inc="VW" if sc4 == 3 else None)
                p1_done.append(("VW", v))

        kd = {0: ka_d, 1: kb_d}
        secbase = {0: 0, 1: NA}
        seclen = {0: NA, 1: NB}
        G = [0]
        vh_hist = []
        sr_hist = []

        for sec in ((0,) if KSEC == "0" else (0, 1)):
            sec0_ts = b.cnt["TS"]
            for sl in range(seclen[sec]):
                s = secbase[sec] + sl
                par = s % 2
                diag = (sl == 0)
                w = [("DQS", CONSTS)]
                if s >= 2:
                    w += [("VH", vh_hist[s - 2]), ("SR", sr_hist[s - 2])]
                src = kd[sec][sl * BS:(sl + 1) * BS, :].rearrange(
                    "(j p) d -> p j d", p=128)
                dst = xraw[par].ap().rearrange("p (j d) -> p j d", j=4)
                kx = "KX0" if par == 0 else "KX1"
                b.on("sync",
                     (lambda d_, s_: lambda e: e.dma_start(out=d_, in_=s_))(dst, src),
                     waits=w, inc=kx, inck=16)
                kx_slot = (kx, b.cnt[kx])

                for i in range(4):
                    xt = xraw[par][:, i * D:(i + 1) * D]
                    c = par * 16 + i * 4
                    b.on("vector",
                         (lambda xt_, c_: lambda e: e.reduce_sum(
                             stc[:, c_:c_ + 1], xt_, AX.X))(xt, c),
                         waits=[kx_slot, ("DQS", CONSTS)], inc="VX")
                    b.on("vector",
                         (lambda c_: lambda e: e.tensor_scalar(
                             stc[:, c_ + 1:c_ + 2], stc[:, c_:c_ + 1],
                             -1.0 / D, None, ALU.mult))(c),
                         waits=[("VX", b.cnt["VX"])], inc="VX")
                    b.on("vector",
                         (lambda c_: lambda e: e.tensor_scalar(
                             stc[:, c_ + 2:c_ + 3], stc[:, c_ + 1:c_ + 2],
                             stc[:, c_ + 1:c_ + 2], -1.0, ALU.mult,
                             ALU.mult))(c),
                         waits=[("VX", b.cnt["VX"])],
                         inc="VS" if i == 3 else None)
                vs_slot = b.cnt["VS"]
                for i in range(4):
                    xt = xraw[par][:, i * D:(i + 1) * D]
                    c = par * 16 + i * 4
                    b.on("scalar",
                         (lambda xt_, c_: lambda e: e.activation(
                             sqscr[:, :], xt_, AF.Square,
                             accum_out=stc[:, c_ + 3:c_ + 4]))(xt, c),
                         waits=[kx_slot, ("DQS", CONSTS)])
                for i in range(4):
                    c = par * 16 + i * 4
                    b.on("scalar",
                         (lambda c_: lambda e: e.activation(
                             stc[:, c_ + 3:c_ + 4], stc[:, c_ + 3:c_ + 4],
                             AF.Relu, bias=stc[:, c_ + 2:c_ + 3],
                             scale=1.0 / D))(c),
                         waits=[("VS", vs_slot)])
                    b.on("scalar",
                         (lambda c_: lambda e: e.activation(
                             stc[:, c_ + 3:c_ + 4], stc[:, c_ + 3:c_ + 4],
                             AF.Ln, bias=EPS_LN))(c))
                    b.on("scalar",
                         (lambda c_: lambda e: e.activation(
                             stc[:, c_ + 3:c_ + 4], stc[:, c_ + 3:c_ + 4],
                             AF.Exp, scale=-0.5))(c),
                         inc="SR" if i == 3 else None)
                sr_hist.append(b.cnt["SR"])
                for i in range(4):
                    xt = xraw[par][:, i * D:(i + 1) * D]
                    ht = hT[par][:, i * D:(i + 1) * D]
                    c = par * 16 + i * 4
                    b.on("vector",
                         (lambda xt_, ht_, c_: lambda e: e.tensor_scalar(
                             ht_, xt_, stc[:, c_ + 1:c_ + 2],
                             stc[:, c_ + 3:c_ + 4], ALU.add, ALU.mult))(xt, ht, c),
                         waits=[("SR", sr_hist[-1])],
                         inc="VH" if i == 3 else None)
                vh_hist.append(b.cnt["VH"])
                vh_slot = b.cnt["VH"]

                p1_emit(8)

                def emit_trans(h_t, par_t, vh_w):
                    if "t" in KSKIP:
                        return
                    for ch in range(4):
                        b.on("tensor",
                             (lambda ch_, p_, h_: lambda e: e.transpose(
                                 tpsum[:, ch_ * 128:(ch_ + 1) * 128],
                                 hT[p_][:, ch_ * D + h_ * 128:
                                         ch_ * D + h_ * 128 + 128],
                                 idb[:, :]))(ch, par_t, h_t),
                             waits=[("VH", vh_w), ("VD", b.cnt["VD"]),
                                    ("DQS", CONSTS)],
                             inc="TT" if ch == 3 else None)
                    b.on("vector",
                         (lambda hp_: lambda e: e.tensor_copy(
                             hDj[hp_][:, :], tpsum[:, :]))(h_t % 2),
                         waits=[("TT", b.cnt["TT"]),
                                ("TS", max(0, b.cnt["TS"] - 2))],
                         inc="VD")
                    if diag:
                        b.on("vector",
                             (lambda h_: lambda e: e.tensor_copy(
                                 hQ[:, h_ * 512:(h_ + 1) * 512],
                                 hDj[h_ % 2][:, :]))(h_t),
                             waits=[("TS", sec0_ts)] if sec == 1 else [],
                             inc="VQ")

                for h in range(H):
                    hp = h % 2
                    if h == 0:
                        emit_trans(0, par, vh_slot)
                    vd_h = b.cnt["VD"]
                    vq_need = sec * H + h + 1
                    to_head = b.cnt["TO"]
                    tl_head = b.cnt["TL"]
                    voa_head = b.cnt["VOA"]
                    vla_head = b.cnt["VLA"]
                    gidx = []
                    for g in ([] if "s" in KSKIP else range(2)):
                        Gi = G[0]
                        G[0] += 1
                        gidx.append(Gi)
                        gp = Gi % 2
                        for k in range(2):
                            ch = 2 * g + k
                            b.on("tensor",
                                 (lambda gp_, k_, ch_, hp_, h_: lambda e: e.matmul(
                                     sps[gp_][:, k_ * 512:(k_ + 1) * 512],
                                     hDj[hp_][:, ch_ * 128:(ch_ + 1) * 128],
                                     hQ[:, h_ * 512:(h_ + 1) * 512],
                                     start=True, stop=True,
                                     skip_group_check=True))(gp, k, ch, hp, h),
                                 waits=[("VD", vd_h), ("VQ", vq_need),
                                        ("SE", Gi - 1)],
                                 inc="TS" if k == 1 else None)
                        ts_g = b.cnt["TS"]
                        b.on("scalar",
                             (lambda gp_: lambda e: e.activation(
                                 eb[gp_][:, :], sps[gp_][:, :], AF.Exp,
                                 scale=SC_SCORE))(gp),
                             waits=[("TS", ts_g), ("TO", to_head),
                                    ("TL", tl_head)],
                             inc="SE")
                        if diag and "mask" not in KSKIP:
                            b.on("vector",
                                 (lambda gp_, g_: lambda e: e.tensor_mul(
                                     eb[gp_][:, :], eb[gp_][:, :],
                                     masks[g_][:, :]))(gp, g),
                                 waits=[("SE", b.cnt["SE"])], inc="VM")
                    if h + 1 < H:
                        emit_trans(h + 1, par, vh_slot)
                    for g in ([] if ("s" in KSKIP or "o" in KSKIP)
                              else range(2)):
                        gp = gidx[g] % 2
                        edep = (("VM", 2 * (b.cnt["VM"] // 2) - (1 - g))
                                if diag else ("SE", b.cnt["SE"] - (1 - g)))
                        for k in range(2):
                            ch = 2 * g + k
                            if "o" in KSKIP:
                                continue
                            b.on("tensor",
                                 (lambda gp_, k_, ch_, p_, h_: lambda e: e.matmul(
                                     ops[:, :],
                                     hT[p_][:, ch_ * D + h_ * 128:
                                             ch_ * D + h_ * 128 + 128],
                                     eb[gp_][:, k_ * 512:(k_ + 1) * 512],
                                     start=(ch_ == 0), stop=(ch_ == 3),
                                     skip_group_check=True))(gp, k, ch, par, h),
                                 waits=[edep, ("VOA", voa_head)],
                                 inc="TO" if ch == 3 else None)
                            wcol = 4 * s + ch
                            p0l = 32 * (h % 2)
                            if "l" in KSKIP:
                                continue
                            b.on("tensor",
                                 (lambda gp_, k_, ch_, wc_, p_: lambda e: e.matmul(
                                     lps[p_:p_ + 1, :],
                                     wvs[:, wc_:wc_ + 1],
                                     eb[gp_][:, k_ * 512:(k_ + 1) * 512],
                                     start=(ch_ == 0), stop=(ch_ == 3),
                                     tile_position=(0, p_),
                                     skip_group_check=True))(gp, k, ch, wcol, p0l),
                                 waits=[edep, ("VLA", vla_head)],
                                 inc="TL" if ch == 3 else None)
                    first = (sl == 0)
                    if "o" not in KSKIP:
                        b.on("vector",
                         (lambda h_, f_: lambda e: (
                             e.tensor_copy(osb[:, h_ * 512:(h_ + 1) * 512],
                                           ops[:, :])
                             if f_ else
                             e.tensor_add(osb[:, h_ * 512:(h_ + 1) * 512],
                                          osb[:, h_ * 512:(h_ + 1) * 512],
                                          ops[:, :])))(h, first),
                         waits=[("TO", b.cnt["TO"])], inc="VOA")
                    p0l = 32 * (h % 2)
                    blkl = (h // 2) * 512
                    if "l" not in KSKIP:
                        b.on("vector",
                         (lambda p_, bl_, f_: lambda e: (
                             e.tensor_copy(lsb[p_:p_ + 1, bl_:bl_ + 512],
                                           lps[p_:p_ + 1, :])
                             if f_ else
                             e.tensor_add(lsb[p_:p_ + 1, bl_:bl_ + 512],
                                          lsb[p_:p_ + 1, bl_:bl_ + 512],
                                          lps[p_:p_ + 1, :])))(p0l, blkl, first),
                         waits=[("TL", b.cnt["TL"])], inc="VLA")

            if "tail" in KSKIP:
                continue
            voa_end = b.cnt["VOA"]
            vla_end = b.cnt["VLA"]
            for j in ([] if "ltr" in KSKIP else range(4)):
                for h in range(H):
                    p0l = 32 * (h % 2)
                    blkl = (h // 2) * 512
                    b.on("tensor",
                         (lambda j_, h_, p_, bl_: lambda e: e.matmul(
                             yps[:, 256 + j_ * H + h_:256 + j_ * H + h_ + 1],
                             lsb[p_:p_ + 1, bl_ + j_ * 128:bl_ + (j_ + 1) * 128],
                             onef[p_:p_ + 1, 0:1], start=True, stop=True,
                             skip_group_check=True))(j, h, p0l, blkl),
                         waits=[("VLA", vla_end), ("VYR", b.cnt["VYR"])],
                         inc="TY" if (j == 3 and h == H - 1) else None)
            if "recip" not in KSKIP:
                b.on("vector",
                     lambda e: e.tensor_copy(linvr[:, 0:4 * H],
                                             yps[:, 256:256 + 4 * H]),
                     inc="VYC",
                     waits=[("TY", b.cnt["TY"])])
                b.on("scalar",
                     lambda e: e.activation(linv[:, 0:4 * H],
                                            linvr[:, 0:4 * H], AF.Ln),
                     waits=[("VYC", b.cnt["VYC"])])
                b.on("scalar",
                     lambda e: e.activation(linv[:, 0:4 * H],
                                            linv[:, 0:4 * H], AF.Exp,
                                            scale=-1.0),
                     inc="VYR")
            vyr = b.cnt["VYR"]
            gy_hist = []
            for j in ([] if "ytr" in KSKIP else range(4)):
                for h in range(H):
                    yp = (j * H + h) % 2
                    b.on("tensor",
                         (lambda h_, j_, yp_: lambda e: e.transpose(
                             yps[:, yp_ * 128:(yp_ + 1) * 128],
                             osb[:, h_ * 512 + j_ * 128:
                                 h_ * 512 + (j_ + 1) * 128],
                             idf[:, :]))(h, j, yp),
                         waits=[("VOA", voa_end), ("VY", b.cnt["VY"]),
                                ("VYR", vyr)],
                         inc="TY")
                    wy = [("TY", b.cnt["TY"]), ("VYR", vyr)]
                    if sec * 4 + j >= 1:
                        wy.append(("GY", 16 * (sec * 4 + j)))
                    if "yev" in KSKIP:
                        continue
                    b.on("vector",
                         (lambda h_, j_, yp_: lambda e: e.tensor_scalar(
                             ysect[j_ % 2][:, h_ * 128:(h_ + 1) * 128],
                             yps[:, yp_ * 128:(yp_ + 1) * 128],
                             linv[:, j_ * H + h_:j_ * H + h_ + 1], None,
                             ALU.mult))(h, j, yp),
                         waits=wy, inc="VY")
                if "ypk" not in KSKIP:
                    b.on("gpsimd",
                         (lambda j_, sec_: lambda e: e.dma_start(
                             out=ypark[sec_ * 512 + j_ * 128:
                                       sec_ * 512 + (j_ + 1) * 128, :],
                             in_=ysect[j_ % 2][:, :]))(j, sec),
                         waits=[("VY", b.cnt["VY"])], inc="GY", inck=16)

        p1_emit(len(p1_jobs))
        GY_ALL = b.cnt["GY"]

        NW = 4 * NEC
        HW = NW // 2
        acchb = sqscr[:, 0:3 * NW]
        acchf = ysect[0][:, 0:3 * NW]
        acclo = ysect[0][:, 3 * NW:6 * NW]
        if "sc" not in KSKIP:
            b.on("vector",
                 lambda e: e.tensor_copy(acchb[:, :], accw[:, :]),
                 waits=[("VW", b.cnt["VW"]), ("SW", b.cnt["SW"]),
                        ("SR", b.cnt["SR"]), ("GY", GY_ALL)], inc="VX")
            b.on("vector",
                 lambda e: e.tensor_copy(acchf[:, :], acchb[:, :]),
                 waits=[("VX", b.cnt["VX"])], inc="VX")
            b.on("vector",
                 lambda e: e.tensor_sub(acclo[:, :], accw[:, :], acchf[:, :]),
                 waits=[("VX", b.cnt["VX"])], inc="VX")
        for k in ([] if "sc" in KSKIP else range(3)):
            b.on("vector",
                 (lambda k_: lambda e: e.reduce_sum(
                     acc3[:, k_:k_ + 1], acchf[:, k_ * NW:k_ * NW + HW],
                     AX.X))(k),
                 waits=[("VX", b.cnt["VX"])])
            b.on("vector",
                 (lambda k_: lambda e: e.reduce_sum(
                     acc3[:, 3 + k_:4 + k_],
                     acchf[:, k_ * NW + HW:(k_ + 1) * NW], AX.X))(k))
            b.on("vector",
                 (lambda k_: lambda e: e.reduce_sum(
                     acc3[:, 6 + k_:7 + k_], acclo[:, k_ * NW:(k_ + 1) * NW],
                     AX.X))(k),
                 inc="VS" if k == 2 else None)
        vs_acc = b.cnt["VS"]
        for k in ([] if "sc" in KSKIP else range(3)):
            for jj, cc in enumerate((k, 3 + k, 6 + k)):
                b.on("tensor",
                     (lambda k_, cc_, jj_: lambda e: e.matmul(
                         yps[0:1, k_:k_ + 1], acc3[:, cc_:cc_ + 1],
                         onef[:, :], start=(jj_ == 0), stop=(jj_ == 2),
                         skip_group_check=True))(k, cc, jj),
                     waits=[("VS", vs_acc), ("TY", b.cnt["TY"]),
                            ("VY", b.cnt["VY"])],
                     inc="TSC" if (k == 2 and jj == 2) else None)
        for k in ([] if "sc" in KSKIP else range(3)):
            b.on("scalar",
                 (lambda k_: lambda e: e.activation(
                     sc1[0:1, k_:k_ + 1], yps[0:1, k_:k_ + 1], AF.Copy,
                     scale=1.0 / (DE * D), bias=EPS_Q))(k),
                 waits=[("TSC", b.cnt["TSC"])])
        for k in ([] if "sc" in KSKIP else range(3)):
            hsc = 0.5 * (1.0 - 2.0e-7) if k == 2 else 0.5
            b.on("scalar",
                 (lambda k_, h_: lambda e: e.activation(
                     sc1[0:1, 3 + 2 * k_:4 + 2 * k_], sc1[0:1, k_:k_ + 1],
                     AF.Copy, scale=h_))(k, hsc))
            b.on("scalar",
                 (lambda k_, h_: lambda e: e.activation(
                     sc1[0:1, 4 + 2 * k_:5 + 2 * k_], sc1[0:1, k_:k_ + 1],
                     AF.Copy, scale=-h_))(k, hsc),
                 inc="S1" if k == 2 else None)
        s1c = b.cnt["S1"]
        if "sc" not in KSKIP:
            b.on("vector",
             lambda e: e.tensor_scalar(sc1[0:1, 9:10], sc1[0:1, 1:2],
                                       sc1[0:1, 2:3], 0.5, ALU.mult, ALU.mult),
             waits=[("S1", s1c)])
            b.on("vector",
                 lambda e: e.tensor_scalar(sc1[0:1, 10:11], sc1[0:1, 1:2],
                                           sc1[0:1, 2:3], 1.0, ALU.mult, ALU.mult),
                 inc="V1")
            b.on("gpsimd",
                 lambda e: e.dma_start(out=scb[0:1, 0:11], in_=sc1[0:1, 0:11]),
                 waits=[("V1", b.cnt["V1"])], inc="SCD", inck=16)
            for lo, hi in ((1, 2), (2, 4), (4, 8), (8, 16), (16, 32), (32, 64),
                           (64, 128)):
                b.on("gpsimd",
                     (lambda lo_, hi_: lambda e: e.dma_start(
                         out=scb[lo_:hi_, 0:11], in_=scb[0:hi_ - lo_, 0:11]))(lo, hi),
                     waits=[("SCD", b.cnt["SCD"])], inc="SCD", inck=16)
        PH1_SCD = b.cnt["SCD"]

        if stages == 1:
            if KDBG:
                wall = [("GY", b.cnt["GY"]), ("VY", b.cnt["VY"]),
                        ("TY", b.cnt["TY"]), ("VYR", b.cnt["VYR"]),
                        ("VOA", b.cnt["VOA"]), ("VLA", b.cnt["VLA"]),
                        ("VH", b.cnt["VH"]), ("VQ", b.cnt["VQ"]),
                        ("SE", b.cnt["SE"])]
                b.on("gpsimd",
                     lambda e: e.dma_start(out=dbgf_d[:, 0:PS_O],
                                           in_=osb[:, :]),
                     waits=wall, inc="OUT", inck=16)
                b.on("gpsimd",
                     lambda e: e.dma_start(out=dbgf_d[0:1, PS_O:PS_O + PS_L],
                                           in_=lsb[0:1, :]),
                     inc="OUT", inck=16)
                if H >= 2:
                    b.on("gpsimd",
                         lambda e: e.dma_start(
                             out=dbgf_d[32:33, PS_O:PS_O + PS_L],
                             in_=lsb[32:33, :]),
                         inc="OUT", inck=16)
                b.on("gpsimd",
                     lambda e: e.dma_start(
                         out=dbgf_d[:, PS_O + PS_L:PS_O + PS_L + 4 * H],
                         in_=linvr[:, 0:4 * H]),
                     inc="OUT", inck=16)
                b.on("gpsimd",
                     lambda e: e.dma_start(
                         out=dbgf_d[:, PS_O + PS_L + 64:PS_O + PS_L + 64 + 4 * H],
                         in_=linv[:, 0:4 * H]),
                     inc="OUT", inck=16)
                b.on("gpsimd",
                     lambda e: e.dma_start(
                         out=dbgf_d[:, PS_O + PS_L + 128:PS_O + PS_L + 144],
                         in_=stc[:, 0:16]),
                     inc="OUT", inck=16)
                b.on("gpsimd",
                     lambda e: e.dma_start(
                         out=dbgf_d[:, PS_O + PS_L + 192:
                                    PS_O + PS_L + 192 + 16 * H],
                         in_=xraw[0][:, 0:16 * H]),
                     inc="OUT", inck=16)
                b.on("gpsimd",
                     lambda e: e.dma_start(out=dbgb_d[:, 0:4 * D],
                                           in_=hT[0][:, :]),
                     inc="OUT", inck=16)
                if (NA if KSEC == "0" else NSLOT) >= 2:
                    b.on("gpsimd",
                         lambda e: e.dma_start(out=dbgb_d[:, 4 * D:8 * D],
                                               in_=hT[1][:, :]),
                         inc="OUT", inck=16)
                b.on("gpsimd",
                     lambda e: e.dma_start(
                         out=dbgb_d[:, 8 * D:8 * D + H * 512],
                         in_=hQ[:, :]),
                     inc="OUT", inck=16)
                b.on("gpsimd",
                     lambda e: e.dma_start(
                         out=dbgb_d[:, 8 * D + H * 512:8 * D + H * 512 + 1024],
                         in_=eb[0][:, :]),
                     inc="OUT", inck=16)
                b.on("gpsimd",
                     lambda e: e.dma_start(
                         out=dbgb_d[:, 8 * D + H * 512 + 1024:
                                    8 * D + H * 512 + 2048],
                         in_=eb[1][:, :]),
                     inc="OUT", inck=16)
            NROW = BS if KSEC == "0" else NT
            b.on("gpsimd",
                 lambda e: e.dma_start(out=out_d[0:NROW, :],
                                       in_=ypark[0:NROW, :]),
                 waits=[("GY", b.cnt["GY"]), ("SCD", b.cnt["SCD"])],
                 inc="OUT", inck=16)
            b.on("gpsimd", lambda e: e.nop(), waits=[("OUT", b.cnt["OUT"])])
        with nc.Block() as blka:
            blka.sync(lambda e: b.run("sync", e, S))
            blka.gpsimd(lambda e: b.run("gpsimd", e, S))
            blka.scalar(lambda e: b.run("scalar", e, S))
            blka.vector(lambda e: b.run("vector", e, S))
            blka.tensor(lambda e: b.run("tensor", e, S))
    if stages == 1:
        es.close()
        return nc

    b = Sched(base=b.cnt)
    with ExitStack() as ph1b:
        def sb1(name, shape, dt):
            return ph1b.enter_context(nc.sbuf_tensor(name, shape, dt))

        ybuf = [sb1(f"ybuf{i}", [128, D], F32) for i in range(2)]
        xqb = [sb1(f"xqb{i}", [128, D], F32) for i in range(2)]
        t1b = sb1("t1b", [128, D], F32)
        x1b = [sb1(f"x1b{i}", [128, D], F32) for i in range(2)]
        h2b = [sb1(f"h2b{i}", [128, D], F32) for i in range(2)]
        sqscr = sb1("sqscr1b", [128, D], BF16)
        stc = sb1("stc1b", [128, 32], F32)

        v1_hist = []
        g1_hist = []
        gh2_hist = []

        def stats_ln(buf, c, extra_w):
            b.on("vector",
                 (lambda b_, c_: lambda e: e.reduce_sum(
                     stc[:, c_:c_ + 1], b_[:, :], AX.X))(buf, c),
                 waits=list(extra_w), inc="VX")
            b.on("vector",
                 (lambda c_: lambda e: e.tensor_scalar(
                     stc[:, c_ + 1:c_ + 2], stc[:, c_:c_ + 1],
                     -1.0 / D, None, ALU.mult))(c),
                 waits=[("VX", b.cnt["VX"])], inc="VX")
            b.on("vector",
                 (lambda c_: lambda e: e.tensor_scalar(
                     stc[:, c_ + 2:c_ + 3], stc[:, c_ + 1:c_ + 2],
                     stc[:, c_ + 1:c_ + 2], -1.0, ALU.mult, ALU.mult))(c),
                 waits=[("VX", b.cnt["VX"])], inc="VS")
            vsv = b.cnt["VS"]
            b.on("scalar",
                 (lambda b_, c_: lambda e: e.activation(
                     sqscr[:, :], b_[:, :], AF.Square,
                     accum_out=stc[:, c_ + 3:c_ + 4]))(buf, c),
                 waits=list(extra_w))
            b.on("scalar",
                 (lambda c_: lambda e: e.activation(
                     stc[:, c_ + 3:c_ + 4], stc[:, c_ + 3:c_ + 4], AF.Relu,
                     bias=stc[:, c_ + 2:c_ + 3], scale=1.0 / D))(c),
                 waits=[("VS", vsv)])
            b.on("scalar",
                 (lambda c_: lambda e: e.activation(
                     stc[:, c_ + 3:c_ + 4], stc[:, c_ + 3:c_ + 4],
                     AF.Ln, bias=EPS_LN))(c))
            b.on("scalar",
                 (lambda c_: lambda e: e.activation(
                     stc[:, c_ + 3:c_ + 4], stc[:, c_ + 3:c_ + 4],
                     AF.Exp, scale=-0.5))(c),
                 inc="SR")
            return b.cnt["SR"]

        for i in range(8):
            ip = i % 2
            wy = [("GY", GY_ALL), ("SCD", PH1_SCD)]
            if i >= 2:
                wy.append(v1_hist[i - 2])
            pp = "P0" if ip == 0 else "P1"
            b.on("sync",
                 (lambda i_, ip_: lambda e: e.dma_start(
                     out=ybuf[ip_][:, :],
                     in_=ypark[i_ * 128:(i_ + 1) * 128, :]))(i, ip),
                 waits=wy, inc=pp, inck=16)
            b.on("sync",
                 (lambda i_, ip_: lambda e: e.dma_start(
                     out=xqb[ip_][:, :],
                     in_=xq_d[i_ * 128:(i_ + 1) * 128, :]))(i, ip),
                 inc=pp, inck=16)
            dq = (pp, b.cnt[pp])
            srv = stats_ln(ybuf[ip], ip * 16, [dq])
            b.on("vector",
                 (lambda ip_, c_: lambda e: e.tensor_scalar(
                     t1b[:, :], ybuf[ip_][:, :], stc[:, c_ + 1:c_ + 2],
                     stc[:, c_ + 3:c_ + 4], ALU.add, ALU.mult))(ip, ip * 16),
                 waits=[("SR", srv)])
            wx1 = [dq]
            if i >= 2:
                wx1 += [g1_hist[i - 2], gh2_hist[i - 2]]
            b.on("vector",
                 (lambda ip_: lambda e: e.tensor_add(
                     x1b[ip_][:, :], t1b[:, :], xqb[ip_][:, :]))(ip),
                 waits=wx1, inc="V1")
            v1_hist.append(("V1", b.cnt["V1"]))
            b.on("gpsimd",
                 (lambda i_, ip_: lambda e: e.dma_start(
                     out=x1park[i_ * 128:(i_ + 1) * 128, :],
                     in_=x1b[ip_][:, :]))(i, ip),
                 waits=[("V1", b.cnt["V1"])], inc="G1", inck=16)
            g1_hist.append(("G1", b.cnt["G1"]))
            srv2 = stats_ln(x1b[ip], ip * 16 + 8, [("V1", b.cnt["V1"])])
            b.on("vector",
                 (lambda ip_, c_: lambda e: e.tensor_scalar(
                     h2b[ip_][:, :], x1b[ip_][:, :], stc[:, c_ + 1:c_ + 2],
                     stc[:, c_ + 3:c_ + 4], ALU.add, ALU.mult))(ip, ip * 16 + 8),
                 waits=[("SR", srv2)], inc="V1")
            b.on("gpsimd",
                 (lambda i_, ip_: lambda e: e.dma_start(
                     out=h2park[i_ * 128:(i_ + 1) * 128, :],
                     in_=h2b[ip_][:, :]))(i, ip),
                 waits=[("V1", b.cnt["V1"])], inc="GH2", inck=16)
            gh2_hist.append(("GH2", b.cnt["GH2"]))

        if stages == 2:
            b.on("gpsimd",
                 lambda e: e.dma_start(out=out_d[:, :], in_=x1park[:, :]),
                 waits=[("G1", b.cnt["G1"]), ("GH2", b.cnt["GH2"])],
                 inc="OUT", inck=16)
            b.on("gpsimd", lambda e: e.nop(), waits=[("OUT", b.cnt["OUT"])])
        with nc.Block() as blkb:
            blkb.sync(lambda e: b.run("sync", e, S))
            blkb.gpsimd(lambda e: b.run("gpsimd", e, S))
            blkb.scalar(lambda e: b.run("scalar", e, S))
            blkb.vector(lambda e: b.run("vector", e, S))
            blkb.tensor(lambda e: b.run("tensor", e, S))
    if stages == 2:
        es.close()
        return nc

    b2 = Sched(base=b.cnt)
    B2W = [("GH2", b.cnt["GH2"]), ("G1", b.cnt["G1"]), ("SCD", PH1_SCD)]
    FQ_PER_EC = 2

    with ExitStack() as ph2:
        def sb2(name, shape, dt):
            return ph2.enter_context(nc.sbuf_tensor(name, shape, dt))

        hid = sb2("hid", [128, NEC * 512], BF16)
        h2T = sb2("h2T", [128, NDS * 512], BF16)
        h2rb = [sb2("h2rb0", [128, D], F32)]
        h2rb = [h2rb[0], h2rb[0]]
        wub = [sb2(f"wub{i}", [128, D], F32) for i in range(2)]
        wgb = [sb2(f"wgb{i}", [128, D], F32) for i in range(2)]
        tub = [sb2(f"tub{i}", [128, D], BF16) for i in range(2)]
        tgb = [sb2(f"tgb{i}", [128, D], BF16) for i in range(2)]
        ssb = sb2("ssb", [128, D], BF16)
        ssb2 = sb2("ssb2", [128, D], BF16)
        silb = [sb2("silb0", [128, 512], F32)]
        silb = [silb[0], silb[0]]
        wdb = [sb2(f"wdb{i}", [128, 512], F32) for i in range(2)]
        tdb = [sb2(f"tdb{i}", [128, 512], BF16) for i in range(2)]
        ffnT = sb2("ffnT", [128, NDS * 512], F32)
        x1rb = [sb2("x1rb0", [128, D], F32)]
        x1rb = [x1rb[0], x1rb[0]]
        t2b = [sb2("t2b0", [128, D], F32)]
        t2b = [t2b[0], t2b[0]]
        sq2 = ssb
        st2 = sb2("st2", [128, 16], F32)

        upg = ph2.enter_context(nc.psum_tensor("upg", [128, 2048], F32))
        dwp = ph2.enter_context(nc.psum_tensor("dwp", [128, 2048], F32))

        th2_hist = []
        for th in range(2):
            tuu_half0 = b2.cnt["TUU"]
            tug_half0 = b2.cnt["TUG"]
            ss_half0 = b2.cnt["SS"]
            vhid_half0 = b2.cnt["VHID"]
            td_half0 = b2.cnt["TD"]
            vf_half0 = b2.cnt["VF"]
            sf_half0 = b2.cnt["SF"]

            for i in ([] if "f2h" in KSKIP else range(4)):
                ip = i % 2
                w = list(B2W)
                if len(th2_hist) >= 1:
                    w.append(th2_hist[-1])
                b2.on("sync",
                      (lambda th_, i_, ip_: lambda e: e.dma_start(
                          out=h2rb[ip_][:, :],
                          in_=h2park[th_ * 512 + i_ * 128:
                                     th_ * 512 + (i_ + 1) * 128, :]))(th, i, ip),
                      waits=w, inc="H2R", inck=16)
                dq = b2.cnt["H2R"]
                for s4 in range(0, NDS, 4):
                    n4 = min(4, NDS - s4)
                    for k in range(n4):
                        s_ = s4 + k
                        w = [("H2R", dq), ("VH2", b2.cnt["VH2"])]
                        if th == 1:
                            w += [("SS", ss_half0), ("VHID", vhid_half0),
                                  ("VF", vf_half0), ("SF", sf_half0)]
                        b2.on("tensor",
                              (lambda ip_, s__, k_: lambda e: e.transpose(
                                  upg[:, k_ * 128:(k_ + 1) * 128],
                                  h2rb[ip_][:, s__ * 128:(s__ + 1) * 128],
                                  idf[:, :]))(ip, s_, k),
                              waits=w, inc="TH2" if k == n4 - 1 else None)
                    dst = h2T.ap().rearrange("p (s t) -> p s t", s=NDS)[
                        :, s4:s4 + n4, i * 128:(i + 1) * 128]
                    src = upg.ap().rearrange("p (k t) -> p k t", k=16)[
                        :, 0:n4, 0:128]
                    b2.on("vector",
                          (lambda d_, s_: lambda e: e.tensor_copy(d_, s_))(dst, src),
                          waits=[("TH2", b2.cnt["TH2"])], inc="VH2")
                th2_hist.append(("TH2", b2.cnt["TH2"]))
            vh2_half = b2.cnt["VH2"]

            fq_hist = []
            gtu_hist = []
            tuu_hist = []
            tug_hist = []
            ss_hist = []
            vhid_hist = []
            gtu_all = [("GTU", b2.cnt["GTU"]), ("GTU1", b2.cnt["GTU1"])]
            for ec in ([] if "f2w" in KSKIP else range(NEC)):
                ep = ec % 2
                wtfree = ([("TUU", tuu_hist[ec - 1][1]),
                           ("TUG", tug_hist[ec - 1][1])]
                          if ec >= 1 and len(tuu_hist) >= ec else [])
                if th == 0:
                    wfree = [fq_hist[ec - 2]] if (ec >= 2 and "f2q" not in KSKIP) else []
                    wg = list(B2W) + wfree
                    if ec >= 2:
                        wg.append(gtu_hist[ec - 2])
                    b2.on("gpsimd",
                          (lambda ec_, ep_: lambda e: e.dma_start(
                              out=wub[ep_][:, :], in_=wup_d[ec_, :, :]))(ec, ep),
                          waits=wg, inc="DQG", inck=16)
                    b2.on("gpsimd",
                          (lambda ec_, ep_: lambda e: e.dma_start(
                              out=wgb[ep_][:, :], in_=wgt_d[ec_, :, :]))(ec, ep),
                          inc="DQG", inck=16)
                    dq = b2.cnt["DQG"]
                    if "f2q" not in KSKIP and "f2qs" not in KSKIP:
                     b2.on("scalar",
                          (lambda ep_: lambda e: e.activation(
                              tub[ep_][:, :], wub[ep_][:, :], AF.Sign,
                              bias=scb[:, 4:5]))(ep),
                          waits=[("DQG", dq), ("SCD", PH1_SCD)] + wtfree)
                    if "f2q" not in KSKIP and "f2qs" not in KSKIP:
                     b2.on("scalar",
                          (lambda ep_: lambda e: e.activation(
                              ssb[:, :], wub[ep_][:, :], AF.Sign,
                              bias=scb[:, 3:4]))(ep), inc="FQS")
                     b2.on("vector",
                          (lambda ep_: lambda e: e.tensor_add(
                              tub[ep_][:, :], tub[ep_][:, :], ssb[:, :]))(ep),
                          waits=wtfree + [("FQS", b2.cnt["FQS"])], inc="FQ")
                    if "f2q" not in KSKIP and "f2qv" not in KSKIP:
                     b2.on("vector",
                          (lambda ep_: lambda e: e.tensor_single_scalar(
                              tgb[ep_][:, :], wgb[ep_][:, :], scb[:, 5:6],
                              ALU.is_gt))(ep),
                          waits=[("DQG", dq), ("SCD", PH1_SCD)])
                     b2.on("vector",
                          (lambda ep_: lambda e: e.tensor_single_scalar(
                              ssb2[:, :], wgb[ep_][:, :], scb[:, 6:7],
                              ALU.is_lt))(ep))
                     b2.on("vector",
                          (lambda ep_: lambda e: e.tensor_sub(
                              tgb[ep_][:, :], tgb[ep_][:, :], ssb2[:, :]))(ep),
                          inc="FQ")
                    fq = b2.cnt["FQ"]
                    fq_hist.append(("FQ", fq))
                    gtup = "GTU" if ep == 0 else "GTU1"
                    if "f2p" not in KSKIP:
                        b2.on("gpsimd",
                              (lambda ec_, ep_: lambda e: e.dma_start(
                                  out=tqu_park[ec_, :, :], in_=tub[ep_][:, :]))(ec, ep),
                              waits=[("FQ", fq)], inc=gtup, inck=16)
                        b2.on("gpsimd",
                              (lambda ec_, ep_: lambda e: e.dma_start(
                                  out=tqg_park[ec_, :, :], in_=tgb[ep_][:, :]))(ec, ep),
                              inc=gtup, inck=16)
                    gtu_hist.append((gtup, b2.cnt[gtup]))
                    mmw = [("FQ", fq)]
                else:
                    if "f2p" in KSKIP:
                        continue
                    fwdp = "FWD" if ep == 0 else "FWD1"
                    wg = list(B2W) + wtfree + gtu_all
                    b2.on("gpsimd",
                          (lambda ec_, ep_: lambda e: e.dma_start(
                              out=tub[ep_][:, :], in_=tqu_park[ec_, :, :]))(ec, ep),
                          waits=wg, inc=fwdp, inck=16)
                    b2.on("gpsimd",
                          (lambda ec_, ep_: lambda e: e.dma_start(
                              out=tgb[ep_][:, :], in_=tqg_park[ec_, :, :]))(ec, ep),
                          inc=fwdp, inck=16)
                    mmw = [(fwdp, b2.cnt[fwdp])]
                for s_ in ([] if "f2u" in KSKIP else range(NDS)):
                    b2.on("tensor",
                          (lambda ep_, s__, f: lambda e: e.matmul(
                              upg[:, ep_ * 512:(ep_ + 1) * 512],
                              tub[ep_][:, s__ * 128:(s__ + 1) * 128],
                              h2T[:, s__ * 512:(s__ + 1) * 512],
                              start=(s__ == 0), stop=f,
                              skip_group_check=True))(ep, s_, s_ == NDS - 1),
                          waits=mmw + [("VH2", vh2_half)]
                          + ([("SS", ss_hist[ec - 1][1])]
                             if ec >= 1 and len(ss_hist) >= ec else []),
                          inc="TUU" if s_ == NDS - 1 else None)
                for s_ in ([] if "f2u" in KSKIP else range(NDS)):
                    b2.on("tensor",
                          (lambda ep_, s__, f: lambda e: e.matmul(
                              upg[:, 1024 + ep_ * 512:1024 + (ep_ + 1) * 512],
                              tgb[ep_][:, s__ * 128:(s__ + 1) * 128],
                              h2T[:, s__ * 512:(s__ + 1) * 512],
                              start=(s__ == 0), stop=f,
                              skip_group_check=True))(ep, s_, s_ == NDS - 1),
                          waits=([("VHID", vhid_hist[ec - 1][1])]
                                 if ec >= 1 and len(vhid_hist) >= ec else []),
                          inc="TUG" if s_ == NDS - 1 else None)
                if "f2u" in KSKIP:
                    continue
                b2.on("scalar",
                      (lambda ep_: lambda e: e.activation(
                          silb[ep_][:, :], upg[:, ep_ * 512:(ep_ + 1) * 512],
                          AF.Silu, scale=scb[:, 3:4]))(ep),
                      waits=[("TUU", b2.cnt["TUU"]), ("SCD", PH1_SCD)]
                      + ([("VHID", vhid_hist[ec - 1][1])]
                         if ec >= 1 and len(vhid_hist) >= ec else []),
                      inc="SS")
                hw = [("TUG", b2.cnt["TUG"]), ("SS", b2.cnt["SS"])]
                if th == 1:
                    hw.append(("TD", td_half0))
                b2.on("vector",
                      (lambda ec_, ep_: lambda e: e.scalar_tensor_tensor(
                          hid[:, ec_ * 512:(ec_ + 1) * 512],
                          upg[:, 1024 + ep_ * 512:1024 + (ep_ + 1) * 512],
                          1.0, silb[ep_][:, :], ALU.mult, ALU.mult))(ec, ep),
                      waits=hw, inc="VHID")
                tuu_hist.append(("TUU", b2.cnt["TUU"]))
                tug_hist.append(("TUG", b2.cnt["TUG"]))
                ss_hist.append(("SS", b2.cnt["SS"]))
                vhid_hist.append(("VHID", b2.cnt["VHID"]))
            vhid_half = b2.cnt["VHID"]

            if "f2d" in KSKIP:
                DGS_eff = []
            else:
                DGS_eff = DGS
            vft_base = b2.cnt["VFT"]
            gtd_all = [("GTD", b2.cnt["GTD"]), ("GTD1", b2.cnt["GTD1"])]
            fqd_hist = []
            gtd_hist = []
            didx = 0
            for gi, dg in enumerate(DGS_eff):
                alpha_col = 9 if gi % 2 == 0 else 10
                W = len(dg) * 128
                c0 = dg[0] * 128
                for ec in range(NEC):
                    ep = ec % 2
                    tdw = [("TD", b2.cnt["TD"] - 1)]
                    if th == 0:
                        wfree = [fqd_hist[didx - 2]] if didx >= 2 else []
                        wg = wfree + ([gtd_hist[didx - 2]] if didx >= 2 else [])
                        b2.on("gpsimd",
                              (lambda ec_, ep_, c_, w_: lambda e: e.dma_start(
                                  out=wdb[ep_][:, 0:w_],
                                  in_=wdn_d[ec_ * 128:(ec_ + 1) * 128,
                                            c_:c_ + w_]))(ec, ep, c0, W),
                              waits=wg, inc="DQG", inck=16)
                        dq = b2.cnt["DQG"]
                        if gi % 2 == 0:
                            b2.on("scalar",
                                  (lambda ep_, w_: lambda e: e.activation(
                                      tdb[ep_][:, 0:w_], wdb[ep_][:, 0:w_],
                                      AF.Sign, bias=scb[:, 8:9]))(ep, W),
                                  waits=[("DQG", dq)] + tdw)
                            b2.on("scalar",
                                  (lambda ep_, w_: lambda e: e.activation(
                                      ssb2[:, 0:w_], wdb[ep_][:, 0:w_], AF.Sign,
                                      bias=scb[:, 7:8]))(ep, W), inc="FQS")
                            b2.on("vector",
                                  (lambda ep_, w_: lambda e: e.tensor_add(
                                      tdb[ep_][:, 0:w_], tdb[ep_][:, 0:w_],
                                      ssb2[:, 0:w_]))(ep, W),
                                  waits=tdw + [("FQS", b2.cnt["FQS"])],
                                  inc="FQ")
                        else:
                            b2.on("vector",
                                  (lambda ep_, w_: lambda e: e.tensor_single_scalar(
                                      tdb[ep_][:, 0:w_], wdb[ep_][:, 0:w_],
                                      scb[:, 7:8], ALU.is_gt))(ep, W),
                                  waits=[("DQG", dq)] + tdw)
                            b2.on("vector",
                                  (lambda ep_, w_: lambda e: e.tensor_single_scalar(
                                      ssb2[:, 0:w_], wdb[ep_][:, 0:w_],
                                      scb[:, 8:9], ALU.is_lt))(ep, W))
                            b2.on("vector",
                                  (lambda ep_, w_: lambda e: e.tensor_sub(
                                      tdb[ep_][:, 0:w_], tdb[ep_][:, 0:w_],
                                      ssb2[:, 0:w_]))(ep, W),
                                  inc="FQ")
                        fqd_hist.append(("FQ", b2.cnt["FQ"]))
                        gtdp = "GTD" if didx % 2 == 0 else "GTD1"
                        b2.on("gpsimd",
                              (lambda gi_, ec_, ep_, w_: lambda e: e.dma_start(
                                  out=tqd_park[gi_, ec_, :, 0:w_],
                                  in_=tdb[ep_][:, 0:w_]))(gi, ec, ep, W),
                              waits=[("FQ", b2.cnt["FQ"])], inc=gtdp, inck=16)
                        gtd_hist.append((gtdp, b2.cnt[gtdp]))
                        mmw = [("FQ", b2.cnt["FQ"])]
                    else:
                        fddp = "FDD" if didx % 2 == 0 else "FDD1"
                        b2.on("gpsimd",
                              (lambda gi_, ec_, ep_, w_: lambda e: e.dma_start(
                                  out=tdb[ep_][:, 0:w_],
                                  in_=tqd_park[gi_, ec_, :, 0:w_]))(gi, ec, ep, W),
                              waits=tdw + gtd_all, inc=fddp, inck=16)
                        mmw = [(fddp, b2.cnt[fddp])]
                    didx += 1
                    dmw = mmw + [("VHID", vhid_half),
                                 ("VFT", vft_base + len(dg) * gi)]
                    if th == 1:
                        dmw += [("VF", vf_half0), ("SF", sf_half0)]
                    for k, dc in enumerate(dg):
                        b2.on("tensor",
                              (lambda ep_, k_, ec_: lambda e: e.matmul(
                                  dwp[:, k_ * 512:(k_ + 1) * 512],
                                  tdb[ep_][:, k_ * 128:(k_ + 1) * 128],
                                  hid[:, ec_ * 512:(ec_ + 1) * 512],
                                  start=(ec_ == 0), stop=(ec_ == NEC - 1),
                                  skip_group_check=True))(ep, k, ec),
                              waits=dmw,
                              inc="TD" if k == len(dg) - 1 else None)
                td = b2.cnt["TD"]
                for k, dc in enumerate(dg):
                    b2.on("vector",
                          (lambda k_, dc_, ac: lambda e: e.tensor_scalar(
                              ffnT[:, dc_ * 512:(dc_ + 1) * 512],
                              dwp[:, k_ * 512:(k_ + 1) * 512],
                              scb[:, ac:ac + 1], None, ALU.mult))(k, dc, alpha_col),
                          waits=[("TD", td), ("TF", b2.cnt["TF"])],
                          inc="VFT")

            vft_half = b2.cnt["VFT"]
            for i in ([] if "f2f" in KSKIP else range(4)):
                ip = i % 2
                fps = [dwp, upg][ip]
                for dc in range(NDS):
                    b2.on("tensor",
                          (lambda i_, dc_, f_: lambda e: e.transpose(
                              f_[:, dc_ * 128:(dc_ + 1) * 128],
                              ffnT[:, dc_ * 512 + i_ * 128:
                                   dc_ * 512 + (i_ + 1) * 128],
                              idf[:, :]))(i, dc, fps),
                          waits=[("VFT", vft_half),
                                 ("VF", b2.cnt["VF"] - 1),
                                 ("SF", b2.cnt["SF"] - 1)],
                          inc="TF" if dc == NDS - 1 else None)
                tf = b2.cnt["TF"]
                b2.on("sync",
                      (lambda th_, i_, ip_: lambda e: e.dma_start(
                          out=x1rb[ip_][:, :],
                          in_=x1park[th_ * 512 + i_ * 128:
                                     th_ * 512 + (i_ + 1) * 128, :]))(th, i, ip),
                      waits=[("VF", b2.cnt["VF"])],
                      inc="X1R", inck=16)
                dq = b2.cnt["X1R"]
                fv = fps[:, 0:NDS * 128]
                c2 = ip * 8
                if "f2fv" in KSKIP:
                    b2.on("vector",
                          (lambda ip_: lambda e: e.memset(t2b[ip_][:, :], 0.5))(ip),
                          waits=[("TF", tf)], inc="VF")
                if "f2fv" not in KSKIP:
                    b2.on("vector",
                      (lambda fv_, c_: lambda e: e.reduce_sum(
                          st2[:, c_:c_ + 1], fv_, AX.X))(fv, c2),
                      waits=[("TF", tf), ("SF", b2.cnt["SF"] - 1)], inc="VX")
                if "f2fv" not in KSKIP:
                    b2.on("vector",
                      (lambda c_: lambda e: e.tensor_scalar(
                          st2[:, c_ + 1:c_ + 2], st2[:, c_:c_ + 1],
                          -1.0 / D, None, ALU.mult))(c2),
                      waits=[("VX", b2.cnt["VX"])], inc="VX")
                    b2.on("vector",
                      (lambda c_: lambda e: e.tensor_scalar(
                          st2[:, c_ + 2:c_ + 3], st2[:, c_ + 1:c_ + 2],
                          st2[:, c_ + 1:c_ + 2], -1.0, ALU.mult,
                          ALU.mult))(c2),
                      waits=[("VX", b2.cnt["VX"])], inc="VS")
                vsv = b2.cnt["VS"]
                if "f2fs" not in KSKIP:
                    b2.on("scalar",
                      (lambda fv_, c_: lambda e: e.activation(
                          sq2[:, 0:NDS * 128], fv_, AF.Square,
                          accum_out=st2[:, c_ + 3:c_ + 4]))(fv, c2),
                      waits=[("TF", tf), ("VS", vsv)])
                if "f2fs" not in KSKIP:
                    b2.on("scalar",
                      (lambda c_: lambda e: e.activation(
                          st2[:, c_ + 3:c_ + 4], st2[:, c_ + 3:c_ + 4],
                          AF.Relu, bias=st2[:, c_ + 2:c_ + 3],
                          scale=1.0 / D))(c2),
                      waits=[("VS", vsv)])
                    b2.on("scalar",
                      (lambda c_: lambda e: e.activation(
                          st2[:, c_ + 3:c_ + 4], st2[:, c_ + 3:c_ + 4],
                          AF.Ln, bias=EPS_LN))(c2))
                    b2.on("scalar",
                      (lambda c_: lambda e: e.activation(
                          st2[:, c_ + 3:c_ + 4], st2[:, c_ + 3:c_ + 4],
                          AF.Exp, scale=-0.5))(c2),
                      inc="SF")
                sfv = b2.cnt["SF"]
                if "f2fv" not in KSKIP:
                    b2.on("vector",
                      (lambda ip_, fv_, c_: lambda e: e.tensor_scalar(
                          t2b[ip_][:, :], fv_, st2[:, c_ + 1:c_ + 2],
                          st2[:, c_ + 3:c_ + 4],
                          ALU.add, ALU.mult))(ip, fv, c2),
                      waits=[("SF", sfv), ("OUT", b2.cnt["OUT"])])
                    b2.on("vector",
                      (lambda ip_: lambda e: e.tensor_add(
                          t2b[ip_][:, :], t2b[ip_][:, :], x1rb[ip_][:, :]))(ip),
                      waits=[("X1R", dq)],
                      inc="VF")
                b2.on("gpsimd",
                      (lambda th_, i_, ip_: lambda e: e.dma_start(
                          out=out_d[th_ * 512 + i_ * 128:
                                    th_ * 512 + (i_ + 1) * 128, :],
                          in_=t2b[ip_][:, :]))(th, i, ip),
                      waits=[("VF", b2.cnt["VF"])], inc="OUT", inck=16)

        if KDBG:
            b2.on("gpsimd",
                  lambda e: e.dma_start(out=dbgq_d[:, 0:16], in_=scb[:, :]),
                  waits=[("SCD", PH1_SCD), ("OUT", b2.cnt["OUT"])],
                  inc="OUT", inck=16)
            b2.on("gpsimd",
                  lambda e: e.dma_start(out=dbgt_d[0, :, :],
                                        in_=tqd_park[1, 47, :, :]),
                  waits=[("GTD", b2.cnt["GTD"]), ("GTD1", b2.cnt["GTD1"])],
                  inc="OUT", inck=16)
            b2.on("gpsimd",
                  lambda e: e.dma_start(out=dbgt_d[1, :, :],
                                        in_=tqd_park[3, 34, :, :]),
                  inc="OUT", inck=16)
        b2.on("gpsimd", lambda e: e.nop(), waits=[("OUT", b2.cnt["OUT"])])

        with nc.Block() as blk2:
            blk2.sync(lambda e: b2.run("sync", e, S))
            blk2.gpsimd(lambda e: b2.run("gpsimd", e, S))
            blk2.scalar(lambda e: b2.run("scalar", e, S))
            blk2.vector(lambda e: b2.run("vector", e, S))
            blk2.tensor(lambda e: b2.run("tensor", e, S))

    es.close()
    return nc



def stage_core_inputs(x, c, nblk, H=16, NA=8, NB=16, qa=None, qb=None):
    D = H * 128
    bf = ml_dtypes.bfloat16
    if qa is None:
        qa, qb = c, nblk - 1 - c
    xq = np.concatenate([x[qa * BS:(qa + 1) * BS], x[qb * BS:(qb + 1) * BS]])
    ka = np.zeros((NA * BS, D), np.float32)
    kb = np.zeros((NB * BS, D), np.float32)
    areal = [qa] + list(range(qa))
    breal = [qb] + list(range(qb))
    for sl, blk in enumerate(areal):
        ka[sl * BS:(sl + 1) * BS] = x[blk * BS:(blk + 1) * BS]
    for sl, blk in enumerate(breal):
        kb[sl * BS:(sl + 1) * BS] = x[blk * BS:(blk + 1) * BS]
    wv = np.zeros((128, 4 * (NA + NB)), dtype=bf)
    wv[:, 0:4 * len(areal)] = 1
    wv[:, 4 * NA:4 * NA + 4 * len(breal)] = 1
    return {"xq": np.ascontiguousarray(xq), "ka": ka, "kb": kb, "wv": wv}


def stage_shared(W_up, W_gate, W_down, H=16):
    D = H * 128
    DE = W_up.shape[0]
    NEC = DE // 128
    bf = ml_dtypes.bfloat16
    wup_sw = np.ascontiguousarray(
        W_up.T.reshape(D // 128, 128, NEC, 128).transpose(2, 1, 0, 3)
        .reshape(NEC, 128, D))
    wgt_sw = np.ascontiguousarray(
        W_gate.T.reshape(D // 128, 128, NEC, 128).transpose(2, 1, 0, 3)
        .reshape(NEC, 128, D))
    wdn_T = np.ascontiguousarray(W_down.T)
    m = []
    for g in range(2):
        mg = np.zeros((128, 1024), dtype=bf)
        for cc in range(2):
            ch = 2 * g + cc
            kpos = ch * 128 + np.arange(128)[:, None]
            qpos = np.arange(512)[None, :]
            mg[:, cc * 512:(cc + 1) * 512] = (kpos <= qpos).astype(bf)
        m.append(mg)
    return {"wup": wup_sw, "wgt": wgt_sw, "wdn": wdn_T,
            "m0": m[0], "m1": m[1],
            "cstv": np.tile(np.array([[1e-5, 1e-8, 0.0, 0.0]], np.float32),
                            (128, 1)),
            "idb": np.eye(128, dtype=bf),
            "idf": np.eye(128, dtype=np.float32),
            "onef": np.ones((128, 1), dtype=np.float32)}


_NC_CACHE = {}


def kernel(L0_attn_x, L0_ffn_W_up, L0_ffn_W_gate, L0_ffn_W_down):
    x = np.ascontiguousarray(np.asarray(L0_attn_x)[0])
    L = x.shape[0]
    nblk = L // BS
    shared = stage_shared(np.asarray(L0_ffn_W_up), np.asarray(L0_ffn_W_gate),
                          np.asarray(L0_ffn_W_down))
    in_maps = [dict(stage_core_inputs(x, c, nblk), **shared)
               for c in range(NCORES)]
    if "full" not in _NC_CACHE:
        _NC_CACHE["full"] = build_program()
    res = run_bass_kernel_spmd(_NC_CACHE["full"], in_maps,
                               core_ids=list(range(NCORES)))
    out = np.empty((L, x.shape[1]), np.float32)
    for c in range(NCORES):
        o = res.results[c]["out"]
        qa, qb = c, nblk - 1 - c
        out[qa * BS:(qa + 1) * BS] = o[0:BS]
        out[qb * BS:(qb + 1) * BS] = o[BS:2 * BS]
    return out[None, :, :]


# version 09
# speedup vs baseline: 1.0391x; 1.0391x over previous
import math
import os
import time
from contextlib import ExitStack
KSKIP = set(os.environ.get("KSKIP", "").split(","))
KDBG = os.environ.get("KDBG", "") != ""
KSEC = os.environ.get("KSEC", "")

import numpy as np
import ml_dtypes

import concourse.bass as bass
import concourse.mybir as mybir
from concourse.bass_utils import run_bass_kernel_spmd

F32 = mybir.dt.float32
BF16 = mybir.dt.bfloat16
AF = mybir.ActivationFunctionType
ALU = mybir.AluOpType
AX = mybir.AxisListType

BS = 512
NCORES = 8
SC_SCORE = 1.0 / math.sqrt(128.0)
EPS_LN = 1e-5
EPS_Q = 1e-8

SEMNAMES = ["DQS", "DQG", "DQG1", "KX0", "KX1", "P0", "P1", "X1R", "H2R",
            "FWD1", "FDD1", "GTU1", "GTD1",
            "GY", "G1", "GH2", "GTU", "GTD", "FWD", "FDD", "FQS",
            "VS", "SR", "VH", "TT", "VD", "VQ", "TS", "SE", "VM", "TO", "TL",
            "VOA", "VLA", "TY", "VYR", "VY", "VW", "SW", "SCD", "TSC", "V1",
            "S1", "TH2", "VH2", "FQ", "TUU", "TUG", "SS", "VHID", "TD",
            "VFT", "TF", "SF", "VF", "OUT", "VYC", "VX"]


class Sched:
    ENGINES = ("sync", "gpsimd", "scalar", "vector", "tensor")

    def __init__(self, base=None):
        self.ev = {e: [] for e in self.ENGINES}
        self.cnt = dict(base) if base else {n: 0 for n in SEMNAMES}
        self.lastwait = {}

    def on(self, engine, fn, waits=(), inc=None, inck=1):
        w = []
        for s, v in waits:
            if v <= 0:
                continue
            key = (engine, s)
            if self.lastwait.get(key, -1) < v:
                self.lastwait[key] = v
                w.append((s, v))
        self.ev[engine].append((fn, w, inc, inck))
        if inc is not None:
            self.cnt[inc] += inck
        return self.cnt.get(inc)

    def run(self, engine, eng, S):
        for fn, w, inc, inck in self.ev[engine]:
            for s, v in w:
                eng.wait_ge(S[s], v)
            inst = fn(eng)
            if inc is not None:
                inst.then_inc(S[inc], inck)


def build_program(H=16, NA=8, NB=16, DE=8192, stages=3):
    D = H * 128
    NT = 2 * BS
    NSLOT = NA + NB
    NEC = DE // 128
    NDS = D // 128
    DGS = [list(range(g, min(g + 4, NDS))) for g in range(0, NDS, 4)]

    nc = bass.Bass("TRN2", target_bir_lowering=False,
                   detect_race_conditions=False)

    xq_d = nc.dram_tensor("xq", [NT, D], F32, kind="ExternalInput").ap()
    ka_d = nc.dram_tensor("ka", [NA * BS, D], F32, kind="ExternalInput").ap()
    kb_d = nc.dram_tensor("kb", [NB * BS, D], F32, kind="ExternalInput").ap()
    wv_d = nc.dram_tensor("wv", [128, 4 * NSLOT], BF16, kind="ExternalInput").ap()
    m0_d = nc.dram_tensor("m0", [128, 1024], BF16, kind="ExternalInput").ap()
    m1_d = nc.dram_tensor("m1", [128, 1024], BF16, kind="ExternalInput").ap()
    idb_d = nc.dram_tensor("idb", [128, 128], BF16, kind="ExternalInput").ap()
    idf_d = nc.dram_tensor("idf", [128, 128], F32, kind="ExternalInput").ap()
    onef_d = nc.dram_tensor("onef", [128, 1], F32, kind="ExternalInput").ap()
    cstv_d = nc.dram_tensor("cstv", [128, 4], F32, kind="ExternalInput").ap()
    wup_d = nc.dram_tensor("wup", [NEC, 128, D], F32, kind="ExternalInput").ap()
    wgt_d = nc.dram_tensor("wgt", [NEC, 128, D], F32, kind="ExternalInput").ap()
    wdn_d = nc.dram_tensor("wdn", [DE, D], F32, kind="ExternalInput").ap()
    out_d = nc.dram_tensor("out", [NT, D], F32, kind="ExternalOutput").ap()
    nc.dram_tensor("pad", [1, int(time.time() * 10) % 9973 + 1], F32,
                   kind="ExternalOutput")
    if KDBG:
        PS_O = H * 512
        PS_L = max(1, H // 2) * 512
        dbgf_d = nc.dram_tensor("dbgf", [128, PS_O + PS_L + 192 + 16 * H], F32,
                                kind="ExternalOutput").ap()
        dbgb_d = nc.dram_tensor("dbgb", [128, 8 * D + H * 512 + 2048], BF16,
                                kind="ExternalOutput").ap()
        dbgq_d = nc.dram_tensor("dbgq", [128, 1024], F32,
                                kind="ExternalOutput").ap()
        dbgt_d = nc.dram_tensor("dbgt", [2, 128, 512], BF16,
                                kind="ExternalOutput").ap()

    ypark = nc.dram_tensor("ypark", [NT, D], F32, kind="Internal").ap()
    x1park = nc.dram_tensor("x1park", [NT, D], F32, kind="Internal").ap()
    h2park = nc.dram_tensor("h2park", [NT, D], F32, kind="Internal").ap()
    tqu_park = nc.dram_tensor("tqu_park", [NEC, 128, D], BF16, kind="Internal").ap()
    tqg_park = nc.dram_tensor("tqg_park", [NEC, 128, D], BF16, kind="Internal").ap()
    tqd_park = nc.dram_tensor("tqd_park", [len(DGS), NEC, 128, 512], BF16,
                              kind="Internal").ap()

    es = ExitStack()
    S = {n: es.enter_context(nc.semaphore(n)) for n in SEMNAMES}

    scb = es.enter_context(nc.sbuf_tensor("scb", [128, 16], F32))
    sc1 = es.enter_context(nc.sbuf_tensor("sc1", [1, 16], F32))
    idb = es.enter_context(nc.sbuf_tensor("idb_s", [128, 128], BF16))
    idf = es.enter_context(nc.sbuf_tensor("idf_s", [128, 128], F32))
    onef = es.enter_context(nc.sbuf_tensor("onef_s", [128, 1], F32))
    accw = es.enter_context(nc.sbuf_tensor("accw", [128, 3 * NEC * 4], F32))
    acc3 = es.enter_context(nc.sbuf_tensor("acc3", [128, 12], F32))
    cstv = es.enter_context(nc.sbuf_tensor("cstv_s", [128, 4], F32))
    nc.const_aps.register(F32, EPS_LN, cstv[:, 0:1])
    nc.const_aps.register(F32, EPS_Q, cstv[:, 1:2])

    b = Sched()

    with ExitStack() as ph1:
        def sb(name, shape, dt):
            return ph1.enter_context(nc.sbuf_tensor(name, shape, dt))

        xraw = [sb(f"xraw{i}", [128, 4 * D], F32) for i in range(2)]
        hT = [sb(f"hT{i}", [128, 4 * D], BF16) for i in range(2)]
        hDj = [sb(f"hDj{i}", [128, 512], BF16) for i in range(2)]
        hQ = sb("hQ", [128, H * 512], BF16)
        osb = sb("osb", [128, H * 512], F32)
        lsb = sb("lsb", [128, max(1, H // 2) * 512], F32)
        linv = sb("linv", [128, 4 * H], F32)
        linvr = sb("linvr", [128, 4 * H], F32)
        eb = [sb(f"eb{i}", [128, 1024], BF16) for i in range(2)]
        masks = [sb(f"mask{i}", [128, 1024], BF16) for i in range(2)]
        wvs = sb("wvs", [128, 4 * NSLOT], BF16)
        ysect = [sb("ysect0", [128, D], F32)]
        ysect = [ysect[0], ysect[0]]
        wp1 = [sb(f"wp1_{i}", [128, D], F32) for i in range(2)]
        sqscr = sb("sqscr", [128, D], BF16)
        stc = sb("stc", [128, 32], F32)

        tpsum = ph1.enter_context(nc.psum_tensor("tpsum", [128, 512], BF16))
        sps = [ph1.enter_context(nc.psum_tensor(f"sps{i}", [128, 1024], F32))
               for i in range(2)]
        ops = ph1.enter_context(nc.psum_tensor("ops", [128, 512], F32))
        lps = ph1.enter_context(nc.psum_tensor("lps", [128, 512], F32))
        yps = ph1.enter_context(nc.psum_tensor("yps", [128, 512], F32))

        def ld(dst, src):
            return lambda e: e.dma_start(out=dst, in_=src)

        for dst, src in ((idb[:, :], idb_d), (idf[:, :], idf_d),
                         (onef[:, :], onef_d), (wvs[:, :], wv_d),
                         (cstv[:, :], cstv_d),
                         (masks[0][:, :], m0_d), (masks[1][:, :], m1_d)):
            b.on("sync", ld(dst, src[:, :]), inc="DQS", inck=16)
        CONSTS = b.cnt["DQS"]

        p1_jobs = ([("u", i) for i in range(NEC)]
                   + [("g", i) for i in range(NEC)]
                   + [("d", i) for i in range(NEC)])
        p1_next = [0]
        p1_done = []

        def p1_emit(njobs):
            if "p1" in KSKIP:
                return
            for _ in range(njobs):
                idx = p1_next[0]
                if idx >= len(p1_jobs):
                    return
                kind, i = p1_jobs[idx]
                j = idx % 2
                p1_next[0] += 1
                if kind == "d":
                    srcap = wdn_d[i * 128:(i + 1) * 128, :]
                else:
                    srcap = (wup_d if kind == "u" else wgt_d)[i, :, :]
                col = {"u": 0, "g": NEC, "d": 2 * NEC}[kind] + i
                w = [p1_done[idx - 2]] if idx >= 2 else []
                dqp = "DQG" if j == 0 else "DQG1"
                b.on("gpsimd", ld(wp1[j][:, :], srcap), waits=w,
                     inc=dqp, inck=16)
                dq = (dqp, b.cnt[dqp])
                SW4 = D // 4
                for sc4 in range(4):
                    v = b.on("vector",
                             (lambda jj, cc, s4_: lambda e: e.reduce_sum(
                                 accw[:, 4 * cc + s4_:4 * cc + s4_ + 1],
                                 wp1[jj][:, s4_ * SW4:(s4_ + 1) * SW4], AX.X,
                                 apply_absolute_value=True))(j, col, sc4),
                             waits=[dq], inc="VW" if sc4 == 3 else None)
                p1_done.append(("VW", v))

        kd = {0: ka_d, 1: kb_d}
        secbase = {0: 0, 1: NA}
        seclen = {0: NA, 1: NB}
        G = [0]
        vh_hist = []
        sr_hist = []

        for sec in ((0,) if KSEC == "0" else (0, 1)):
            sec0_ts = b.cnt["TS"]
            for sl in range(seclen[sec]):
                s = secbase[sec] + sl
                par = s % 2
                diag = (sl == 0)
                w = [("DQS", CONSTS)]
                if s >= 2:
                    w += [("VH", vh_hist[s - 2]), ("SR", sr_hist[s - 2])]
                src = kd[sec][sl * BS:(sl + 1) * BS, :].rearrange(
                    "(j p) d -> p j d", p=128)
                dst = xraw[par].ap().rearrange("p (j d) -> p j d", j=4)
                kx = "KX0" if par == 0 else "KX1"
                b.on("sync",
                     (lambda d_, s_: lambda e: e.dma_start(out=d_, in_=s_))(dst, src),
                     waits=w, inc=kx, inck=16)
                kx_slot = (kx, b.cnt[kx])

                for i in range(4):
                    xt = xraw[par][:, i * D:(i + 1) * D]
                    c = par * 16 + i * 4
                    b.on("vector",
                         (lambda xt_, c_: lambda e: e.reduce_sum(
                             stc[:, c_:c_ + 1], xt_, AX.X))(xt, c),
                         waits=[kx_slot, ("DQS", CONSTS)], inc="VX")
                    b.on("vector",
                         (lambda c_: lambda e: e.tensor_scalar(
                             stc[:, c_ + 1:c_ + 2], stc[:, c_:c_ + 1],
                             -1.0 / D, None, ALU.mult))(c),
                         waits=[("VX", b.cnt["VX"])], inc="VX")
                    b.on("vector",
                         (lambda c_: lambda e: e.tensor_scalar(
                             stc[:, c_ + 2:c_ + 3], stc[:, c_ + 1:c_ + 2],
                             stc[:, c_ + 1:c_ + 2], -1.0, ALU.mult,
                             ALU.mult))(c),
                         waits=[("VX", b.cnt["VX"])],
                         inc="VS" if i == 3 else None)
                vs_slot = b.cnt["VS"]
                for i in range(4):
                    xt = xraw[par][:, i * D:(i + 1) * D]
                    c = par * 16 + i * 4
                    b.on("scalar",
                         (lambda xt_, c_: lambda e: e.activation(
                             sqscr[:, :], xt_, AF.Square,
                             accum_out=stc[:, c_ + 3:c_ + 4]))(xt, c),
                         waits=[kx_slot, ("DQS", CONSTS)])
                for i in range(4):
                    c = par * 16 + i * 4
                    b.on("scalar",
                         (lambda c_: lambda e: e.activation(
                             stc[:, c_ + 3:c_ + 4], stc[:, c_ + 3:c_ + 4],
                             AF.Relu, bias=stc[:, c_ + 2:c_ + 3],
                             scale=1.0 / D))(c),
                         waits=[("VS", vs_slot)])
                    b.on("scalar",
                         (lambda c_: lambda e: e.activation(
                             stc[:, c_ + 3:c_ + 4], stc[:, c_ + 3:c_ + 4],
                             AF.Ln, bias=EPS_LN))(c))
                    b.on("scalar",
                         (lambda c_: lambda e: e.activation(
                             stc[:, c_ + 3:c_ + 4], stc[:, c_ + 3:c_ + 4],
                             AF.Exp, scale=-0.5))(c),
                         inc="SR" if i == 3 else None)
                sr_hist.append(b.cnt["SR"])
                for i in range(4):
                    xt = xraw[par][:, i * D:(i + 1) * D]
                    ht = hT[par][:, i * D:(i + 1) * D]
                    c = par * 16 + i * 4
                    b.on("vector",
                         (lambda xt_, ht_, c_: lambda e: e.tensor_scalar(
                             ht_, xt_, stc[:, c_ + 1:c_ + 2],
                             stc[:, c_ + 3:c_ + 4], ALU.add, ALU.mult))(xt, ht, c),
                         waits=[("SR", sr_hist[-1])],
                         inc="VH" if i == 3 else None)
                vh_hist.append(b.cnt["VH"])
                vh_slot = b.cnt["VH"]

                p1_emit(8)

                def emit_trans(h_t, par_t, vh_w):
                    if "t" in KSKIP:
                        return
                    for ch in range(4):
                        b.on("tensor",
                             (lambda ch_, p_, h_: lambda e: e.transpose(
                                 tpsum[:, ch_ * 128:(ch_ + 1) * 128],
                                 hT[p_][:, ch_ * D + h_ * 128:
                                         ch_ * D + h_ * 128 + 128],
                                 idb[:, :]))(ch, par_t, h_t),
                             waits=[("VH", vh_w), ("VD", b.cnt["VD"]),
                                    ("DQS", CONSTS)],
                             inc="TT" if ch == 3 else None)
                    b.on("vector",
                         (lambda hp_: lambda e: e.tensor_copy(
                             hDj[hp_][:, :], tpsum[:, :]))(h_t % 2),
                         waits=[("TT", b.cnt["TT"]),
                                ("TS", max(0, b.cnt["TS"] - 2))],
                         inc="VD")
                    if diag:
                        b.on("vector",
                             (lambda h_: lambda e: e.tensor_copy(
                                 hQ[:, h_ * 512:(h_ + 1) * 512],
                                 hDj[h_ % 2][:, :]))(h_t),
                             waits=[("TS", sec0_ts)] if sec == 1 else [],
                             inc="VQ")

                for h in range(H):
                    hp = h % 2
                    if h == 0:
                        emit_trans(0, par, vh_slot)
                    vd_h = b.cnt["VD"]
                    vq_need = sec * H + h + 1
                    to_head = b.cnt["TO"]
                    tl_head = b.cnt["TL"]
                    voa_head = b.cnt["VOA"]
                    vla_head = b.cnt["VLA"]
                    gidx = []
                    for g in ([] if "s" in KSKIP else range(2)):
                        Gi = G[0]
                        G[0] += 1
                        gidx.append(Gi)
                        gp = Gi % 2
                        for k in range(2):
                            ch = 2 * g + k
                            b.on("tensor",
                                 (lambda gp_, k_, ch_, hp_, h_: lambda e: e.matmul(
                                     sps[gp_][:, k_ * 512:(k_ + 1) * 512],
                                     hDj[hp_][:, ch_ * 128:(ch_ + 1) * 128],
                                     hQ[:, h_ * 512:(h_ + 1) * 512],
                                     start=True, stop=True,
                                     skip_group_check=True))(gp, k, ch, hp, h),
                                 waits=[("VD", vd_h), ("VQ", vq_need),
                                        ("SE", Gi - 1)],
                                 inc="TS" if k == 1 else None)
                        ts_g = b.cnt["TS"]
                        b.on("scalar",
                             (lambda gp_: lambda e: e.activation(
                                 eb[gp_][:, :], sps[gp_][:, :], AF.Exp,
                                 scale=SC_SCORE))(gp),
                             waits=[("TS", ts_g), ("TO", to_head),
                                    ("TL", tl_head)],
                             inc="SE")
                        if diag and "mask" not in KSKIP:
                            b.on("vector",
                                 (lambda gp_, g_: lambda e: e.tensor_mul(
                                     eb[gp_][:, :], eb[gp_][:, :],
                                     masks[g_][:, :]))(gp, g),
                                 waits=[("SE", b.cnt["SE"])], inc="VM")
                    if h + 1 < H:
                        emit_trans(h + 1, par, vh_slot)
                    for g in ([] if ("s" in KSKIP or "o" in KSKIP)
                              else range(2)):
                        gp = gidx[g] % 2
                        edep = (("VM", 2 * (b.cnt["VM"] // 2) - (1 - g))
                                if diag else ("SE", b.cnt["SE"] - (1 - g)))
                        for k in range(2):
                            ch = 2 * g + k
                            if "o" in KSKIP:
                                continue
                            b.on("tensor",
                                 (lambda gp_, k_, ch_, p_, h_: lambda e: e.matmul(
                                     ops[:, :],
                                     hT[p_][:, ch_ * D + h_ * 128:
                                             ch_ * D + h_ * 128 + 128],
                                     eb[gp_][:, k_ * 512:(k_ + 1) * 512],
                                     start=(ch_ == 0), stop=(ch_ == 3),
                                     skip_group_check=True))(gp, k, ch, par, h),
                                 waits=[edep, ("VOA", voa_head)],
                                 inc="TO" if ch == 3 else None)
                            wcol = 4 * s + ch
                            p0l = 32 * (h % 2)
                            if "l" in KSKIP:
                                continue
                            b.on("tensor",
                                 (lambda gp_, k_, ch_, wc_, p_: lambda e: e.matmul(
                                     lps[p_:p_ + 1, :],
                                     wvs[:, wc_:wc_ + 1],
                                     eb[gp_][:, k_ * 512:(k_ + 1) * 512],
                                     start=(ch_ == 0), stop=(ch_ == 3),
                                     tile_position=(0, p_),
                                     skip_group_check=True))(gp, k, ch, wcol, p0l),
                                 waits=[edep, ("VLA", vla_head)],
                                 inc="TL" if ch == 3 else None)
                    first = (sl == 0)
                    if "o" not in KSKIP:
                        b.on("vector",
                         (lambda h_, f_: lambda e: (
                             e.tensor_copy(osb[:, h_ * 512:(h_ + 1) * 512],
                                           ops[:, :])
                             if f_ else
                             e.tensor_add(osb[:, h_ * 512:(h_ + 1) * 512],
                                          osb[:, h_ * 512:(h_ + 1) * 512],
                                          ops[:, :])))(h, first),
                         waits=[("TO", b.cnt["TO"])], inc="VOA")
                    p0l = 32 * (h % 2)
                    blkl = (h // 2) * 512
                    if "l" not in KSKIP:
                        b.on("vector",
                         (lambda p_, bl_, f_: lambda e: (
                             e.tensor_copy(lsb[p_:p_ + 1, bl_:bl_ + 512],
                                           lps[p_:p_ + 1, :])
                             if f_ else
                             e.tensor_add(lsb[p_:p_ + 1, bl_:bl_ + 512],
                                          lsb[p_:p_ + 1, bl_:bl_ + 512],
                                          lps[p_:p_ + 1, :])))(p0l, blkl, first),
                         waits=[("TL", b.cnt["TL"])], inc="VLA")

            if "tail" in KSKIP:
                continue
            voa_end = b.cnt["VOA"]
            vla_end = b.cnt["VLA"]
            for j in ([] if "ltr" in KSKIP else range(4)):
                for h in range(H):
                    p0l = 32 * (h % 2)
                    blkl = (h // 2) * 512
                    b.on("tensor",
                         (lambda j_, h_, p_, bl_: lambda e: e.matmul(
                             yps[:, 256 + j_ * H + h_:256 + j_ * H + h_ + 1],
                             lsb[p_:p_ + 1, bl_ + j_ * 128:bl_ + (j_ + 1) * 128],
                             onef[p_:p_ + 1, 0:1], start=True, stop=True,
                             skip_group_check=True))(j, h, p0l, blkl),
                         waits=[("VLA", vla_end), ("VYR", b.cnt["VYR"])],
                         inc="TY" if (j == 3 and h == H - 1) else None)
            if "recip" not in KSKIP:
                b.on("vector",
                     lambda e: e.tensor_copy(linvr[:, 0:4 * H],
                                             yps[:, 256:256 + 4 * H]),
                     inc="VYC",
                     waits=[("TY", b.cnt["TY"])])
                b.on("scalar",
                     lambda e: e.activation(linv[:, 0:4 * H],
                                            linvr[:, 0:4 * H], AF.Ln),
                     waits=[("VYC", b.cnt["VYC"])])
                b.on("scalar",
                     lambda e: e.activation(linv[:, 0:4 * H],
                                            linv[:, 0:4 * H], AF.Exp,
                                            scale=-1.0),
                     inc="VYR")
            vyr = b.cnt["VYR"]
            gy_hist = []
            for j in ([] if "ytr" in KSKIP else range(4)):
                for h in range(H):
                    yp = (j * H + h) % 2
                    b.on("tensor",
                         (lambda h_, j_, yp_: lambda e: e.transpose(
                             yps[:, yp_ * 128:(yp_ + 1) * 128],
                             osb[:, h_ * 512 + j_ * 128:
                                 h_ * 512 + (j_ + 1) * 128],
                             idf[:, :]))(h, j, yp),
                         waits=[("VOA", voa_end), ("VY", b.cnt["VY"]),
                                ("VYR", vyr)],
                         inc="TY")
                    wy = [("TY", b.cnt["TY"]), ("VYR", vyr)]
                    if sec * 4 + j >= 1:
                        wy.append(("GY", 16 * (sec * 4 + j)))
                    if "yev" in KSKIP:
                        continue
                    b.on("vector",
                         (lambda h_, j_, yp_: lambda e: e.tensor_scalar(
                             ysect[j_ % 2][:, h_ * 128:(h_ + 1) * 128],
                             yps[:, yp_ * 128:(yp_ + 1) * 128],
                             linv[:, j_ * H + h_:j_ * H + h_ + 1], None,
                             ALU.mult))(h, j, yp),
                         waits=wy, inc="VY")
                if "ypk" not in KSKIP:
                    b.on("gpsimd",
                         (lambda j_, sec_: lambda e: e.dma_start(
                             out=ypark[sec_ * 512 + j_ * 128:
                                       sec_ * 512 + (j_ + 1) * 128, :],
                             in_=ysect[j_ % 2][:, :]))(j, sec),
                         waits=[("VY", b.cnt["VY"])], inc="GY", inck=16)

        p1_emit(len(p1_jobs))
        GY_ALL = b.cnt["GY"]

        NW = 4 * NEC
        HW = NW // 2
        acchb = sqscr[:, 0:3 * NW]
        acchf = ysect[0][:, 0:3 * NW]
        acclo = ysect[0][:, 3 * NW:6 * NW]
        if "sc" not in KSKIP:
            b.on("vector",
                 lambda e: e.tensor_copy(acchb[:, :], accw[:, :]),
                 waits=[("VW", b.cnt["VW"]), ("SW", b.cnt["SW"]),
                        ("SR", b.cnt["SR"]), ("GY", GY_ALL)], inc="VX")
            b.on("vector",
                 lambda e: e.tensor_copy(acchf[:, :], acchb[:, :]),
                 waits=[("VX", b.cnt["VX"])], inc="VX")
            b.on("vector",
                 lambda e: e.tensor_sub(acclo[:, :], accw[:, :], acchf[:, :]),
                 waits=[("VX", b.cnt["VX"])], inc="VX")
        for k in ([] if "sc" in KSKIP else range(3)):
            b.on("vector",
                 (lambda k_: lambda e: e.reduce_sum(
                     acc3[:, k_:k_ + 1], acchf[:, k_ * NW:k_ * NW + HW],
                     AX.X))(k),
                 waits=[("VX", b.cnt["VX"])])
            b.on("vector",
                 (lambda k_: lambda e: e.reduce_sum(
                     acc3[:, 3 + k_:4 + k_],
                     acchf[:, k_ * NW + HW:(k_ + 1) * NW], AX.X))(k))
            b.on("vector",
                 (lambda k_: lambda e: e.reduce_sum(
                     acc3[:, 6 + k_:7 + k_], acclo[:, k_ * NW:(k_ + 1) * NW],
                     AX.X))(k),
                 inc="VS" if k == 2 else None)
        vs_acc = b.cnt["VS"]
        for k in ([] if "sc" in KSKIP else range(3)):
            for jj, cc in enumerate((k, 3 + k, 6 + k)):
                b.on("tensor",
                     (lambda k_, cc_, jj_: lambda e: e.matmul(
                         yps[0:1, k_:k_ + 1], acc3[:, cc_:cc_ + 1],
                         onef[:, :], start=(jj_ == 0), stop=(jj_ == 2),
                         skip_group_check=True))(k, cc, jj),
                     waits=[("VS", vs_acc), ("TY", b.cnt["TY"]),
                            ("VY", b.cnt["VY"])],
                     inc="TSC" if (k == 2 and jj == 2) else None)
        for k in ([] if "sc" in KSKIP else range(3)):
            b.on("scalar",
                 (lambda k_: lambda e: e.activation(
                     sc1[0:1, k_:k_ + 1], yps[0:1, k_:k_ + 1], AF.Copy,
                     scale=1.0 / (DE * D), bias=EPS_Q))(k),
                 waits=[("TSC", b.cnt["TSC"])])
        for k in ([] if "sc" in KSKIP else range(3)):
            hsc = 0.5 * (1.0 - 2.0e-7) if k == 2 else 0.5
            b.on("scalar",
                 (lambda k_, h_: lambda e: e.activation(
                     sc1[0:1, 3 + 2 * k_:4 + 2 * k_], sc1[0:1, k_:k_ + 1],
                     AF.Copy, scale=h_))(k, hsc))
            b.on("scalar",
                 (lambda k_, h_: lambda e: e.activation(
                     sc1[0:1, 4 + 2 * k_:5 + 2 * k_], sc1[0:1, k_:k_ + 1],
                     AF.Copy, scale=-h_))(k, hsc),
                 inc="S1" if k == 2 else None)
        s1c = b.cnt["S1"]
        if "sc" not in KSKIP:
            b.on("vector",
             lambda e: e.tensor_scalar(sc1[0:1, 9:10], sc1[0:1, 1:2],
                                       sc1[0:1, 2:3], 0.5, ALU.mult, ALU.mult),
             waits=[("S1", s1c)])
            b.on("vector",
                 lambda e: e.tensor_scalar(sc1[0:1, 10:11], sc1[0:1, 1:2],
                                           sc1[0:1, 2:3], 1.0, ALU.mult, ALU.mult),
                 inc="V1")
            b.on("gpsimd",
                 lambda e: e.dma_start(out=scb[0:1, 0:11], in_=sc1[0:1, 0:11]),
                 waits=[("V1", b.cnt["V1"])], inc="SCD", inck=16)
            for lo, hi in ((1, 2), (2, 4), (4, 8), (8, 16), (16, 32), (32, 64),
                           (64, 128)):
                b.on("gpsimd",
                     (lambda lo_, hi_: lambda e: e.dma_start(
                         out=scb[lo_:hi_, 0:11], in_=scb[0:hi_ - lo_, 0:11]))(lo, hi),
                     waits=[("SCD", b.cnt["SCD"])], inc="SCD", inck=16)
        PH1_SCD = b.cnt["SCD"]

        if stages == 1:
            if KDBG:
                wall = [("GY", b.cnt["GY"]), ("VY", b.cnt["VY"]),
                        ("TY", b.cnt["TY"]), ("VYR", b.cnt["VYR"]),
                        ("VOA", b.cnt["VOA"]), ("VLA", b.cnt["VLA"]),
                        ("VH", b.cnt["VH"]), ("VQ", b.cnt["VQ"]),
                        ("SE", b.cnt["SE"])]
                b.on("gpsimd",
                     lambda e: e.dma_start(out=dbgf_d[:, 0:PS_O],
                                           in_=osb[:, :]),
                     waits=wall, inc="OUT", inck=16)
                b.on("gpsimd",
                     lambda e: e.dma_start(out=dbgf_d[0:1, PS_O:PS_O + PS_L],
                                           in_=lsb[0:1, :]),
                     inc="OUT", inck=16)
                if H >= 2:
                    b.on("gpsimd",
                         lambda e: e.dma_start(
                             out=dbgf_d[32:33, PS_O:PS_O + PS_L],
                             in_=lsb[32:33, :]),
                         inc="OUT", inck=16)
                b.on("gpsimd",
                     lambda e: e.dma_start(
                         out=dbgf_d[:, PS_O + PS_L:PS_O + PS_L + 4 * H],
                         in_=linvr[:, 0:4 * H]),
                     inc="OUT", inck=16)
                b.on("gpsimd",
                     lambda e: e.dma_start(
                         out=dbgf_d[:, PS_O + PS_L + 64:PS_O + PS_L + 64 + 4 * H],
                         in_=linv[:, 0:4 * H]),
                     inc="OUT", inck=16)
                b.on("gpsimd",
                     lambda e: e.dma_start(
                         out=dbgf_d[:, PS_O + PS_L + 128:PS_O + PS_L + 144],
                         in_=stc[:, 0:16]),
                     inc="OUT", inck=16)
                b.on("gpsimd",
                     lambda e: e.dma_start(
                         out=dbgf_d[:, PS_O + PS_L + 192:
                                    PS_O + PS_L + 192 + 16 * H],
                         in_=xraw[0][:, 0:16 * H]),
                     inc="OUT", inck=16)
                b.on("gpsimd",
                     lambda e: e.dma_start(out=dbgb_d[:, 0:4 * D],
                                           in_=hT[0][:, :]),
                     inc="OUT", inck=16)
                if (NA if KSEC == "0" else NSLOT) >= 2:
                    b.on("gpsimd",
                         lambda e: e.dma_start(out=dbgb_d[:, 4 * D:8 * D],
                                               in_=hT[1][:, :]),
                         inc="OUT", inck=16)
                b.on("gpsimd",
                     lambda e: e.dma_start(
                         out=dbgb_d[:, 8 * D:8 * D + H * 512],
                         in_=hQ[:, :]),
                     inc="OUT", inck=16)
                b.on("gpsimd",
                     lambda e: e.dma_start(
                         out=dbgb_d[:, 8 * D + H * 512:8 * D + H * 512 + 1024],
                         in_=eb[0][:, :]),
                     inc="OUT", inck=16)
                b.on("gpsimd",
                     lambda e: e.dma_start(
                         out=dbgb_d[:, 8 * D + H * 512 + 1024:
                                    8 * D + H * 512 + 2048],
                         in_=eb[1][:, :]),
                     inc="OUT", inck=16)
            NROW = BS if KSEC == "0" else NT
            b.on("gpsimd",
                 lambda e: e.dma_start(out=out_d[0:NROW, :],
                                       in_=ypark[0:NROW, :]),
                 waits=[("GY", b.cnt["GY"]), ("SCD", b.cnt["SCD"])],
                 inc="OUT", inck=16)
            b.on("gpsimd", lambda e: e.nop(), waits=[("OUT", b.cnt["OUT"])])
        with nc.Block() as blka:
            blka.sync(lambda e: b.run("sync", e, S))
            blka.gpsimd(lambda e: b.run("gpsimd", e, S))
            blka.scalar(lambda e: b.run("scalar", e, S))
            blka.vector(lambda e: b.run("vector", e, S))
            blka.tensor(lambda e: b.run("tensor", e, S))
    if stages == 1:
        es.close()
        return nc

    b = Sched(base=b.cnt)
    with ExitStack() as ph1b:
        def sb1(name, shape, dt):
            return ph1b.enter_context(nc.sbuf_tensor(name, shape, dt))

        ybuf = [sb1(f"ybuf{i}", [128, D], F32) for i in range(2)]
        xqb = [sb1(f"xqb{i}", [128, D], F32) for i in range(2)]
        t1b = sb1("t1b", [128, D], F32)
        x1b = [sb1(f"x1b{i}", [128, D], F32) for i in range(2)]
        h2b = [sb1(f"h2b{i}", [128, D], F32) for i in range(2)]
        sqscr = sb1("sqscr1b", [128, D], BF16)
        stc = sb1("stc1b", [128, 32], F32)

        v1_hist = []
        g1_hist = []
        gh2_hist = []

        def stats_ln(buf, c, extra_w):
            b.on("vector",
                 (lambda b_, c_: lambda e: e.reduce_sum(
                     stc[:, c_:c_ + 1], b_[:, :], AX.X))(buf, c),
                 waits=list(extra_w), inc="VX")
            b.on("vector",
                 (lambda c_: lambda e: e.tensor_scalar(
                     stc[:, c_ + 1:c_ + 2], stc[:, c_:c_ + 1],
                     -1.0 / D, None, ALU.mult))(c),
                 waits=[("VX", b.cnt["VX"])], inc="VX")
            b.on("vector",
                 (lambda c_: lambda e: e.tensor_scalar(
                     stc[:, c_ + 2:c_ + 3], stc[:, c_ + 1:c_ + 2],
                     stc[:, c_ + 1:c_ + 2], -1.0, ALU.mult, ALU.mult))(c),
                 waits=[("VX", b.cnt["VX"])], inc="VS")
            vsv = b.cnt["VS"]
            b.on("scalar",
                 (lambda b_, c_: lambda e: e.activation(
                     sqscr[:, :], b_[:, :], AF.Square,
                     accum_out=stc[:, c_ + 3:c_ + 4]))(buf, c),
                 waits=list(extra_w))
            b.on("scalar",
                 (lambda c_: lambda e: e.activation(
                     stc[:, c_ + 3:c_ + 4], stc[:, c_ + 3:c_ + 4], AF.Relu,
                     bias=stc[:, c_ + 2:c_ + 3], scale=1.0 / D))(c),
                 waits=[("VS", vsv)])
            b.on("scalar",
                 (lambda c_: lambda e: e.activation(
                     stc[:, c_ + 3:c_ + 4], stc[:, c_ + 3:c_ + 4],
                     AF.Ln, bias=EPS_LN))(c))
            b.on("scalar",
                 (lambda c_: lambda e: e.activation(
                     stc[:, c_ + 3:c_ + 4], stc[:, c_ + 3:c_ + 4],
                     AF.Exp, scale=-0.5))(c),
                 inc="SR")
            return b.cnt["SR"]

        for i in range(8):
            ip = i % 2
            wy = [("GY", GY_ALL), ("SCD", PH1_SCD)]
            if i >= 2:
                wy.append(v1_hist[i - 2])
            pp = "P0" if ip == 0 else "P1"
            b.on("sync",
                 (lambda i_, ip_: lambda e: e.dma_start(
                     out=ybuf[ip_][:, :],
                     in_=ypark[i_ * 128:(i_ + 1) * 128, :]))(i, ip),
                 waits=wy, inc=pp, inck=16)
            b.on("sync",
                 (lambda i_, ip_: lambda e: e.dma_start(
                     out=xqb[ip_][:, :],
                     in_=xq_d[i_ * 128:(i_ + 1) * 128, :]))(i, ip),
                 inc=pp, inck=16)
            dq = (pp, b.cnt[pp])
            srv = stats_ln(ybuf[ip], ip * 16, [dq])
            b.on("vector",
                 (lambda ip_, c_: lambda e: e.tensor_scalar(
                     t1b[:, :], ybuf[ip_][:, :], stc[:, c_ + 1:c_ + 2],
                     stc[:, c_ + 3:c_ + 4], ALU.add, ALU.mult))(ip, ip * 16),
                 waits=[("SR", srv)])
            wx1 = [dq]
            if i >= 2:
                wx1 += [g1_hist[i - 2], gh2_hist[i - 2]]
            b.on("vector",
                 (lambda ip_: lambda e: e.tensor_add(
                     x1b[ip_][:, :], t1b[:, :], xqb[ip_][:, :]))(ip),
                 waits=wx1, inc="V1")
            v1_hist.append(("V1", b.cnt["V1"]))
            b.on("gpsimd",
                 (lambda i_, ip_: lambda e: e.dma_start(
                     out=x1park[i_ * 128:(i_ + 1) * 128, :],
                     in_=x1b[ip_][:, :]))(i, ip),
                 waits=[("V1", b.cnt["V1"])], inc="G1", inck=16)
            g1_hist.append(("G1", b.cnt["G1"]))
            srv2 = stats_ln(x1b[ip], ip * 16 + 8, [("V1", b.cnt["V1"])])
            b.on("vector",
                 (lambda ip_, c_: lambda e: e.tensor_scalar(
                     h2b[ip_][:, :], x1b[ip_][:, :], stc[:, c_ + 1:c_ + 2],
                     stc[:, c_ + 3:c_ + 4], ALU.add, ALU.mult))(ip, ip * 16 + 8),
                 waits=[("SR", srv2)], inc="V1")
            b.on("gpsimd",
                 (lambda i_, ip_: lambda e: e.dma_start(
                     out=h2park[i_ * 128:(i_ + 1) * 128, :],
                     in_=h2b[ip_][:, :]))(i, ip),
                 waits=[("V1", b.cnt["V1"])], inc="GH2", inck=16)
            gh2_hist.append(("GH2", b.cnt["GH2"]))

        if stages == 2:
            b.on("gpsimd",
                 lambda e: e.dma_start(out=out_d[:, :], in_=x1park[:, :]),
                 waits=[("G1", b.cnt["G1"]), ("GH2", b.cnt["GH2"])],
                 inc="OUT", inck=16)
            b.on("gpsimd", lambda e: e.nop(), waits=[("OUT", b.cnt["OUT"])])
        with nc.Block() as blkb:
            blkb.sync(lambda e: b.run("sync", e, S))
            blkb.gpsimd(lambda e: b.run("gpsimd", e, S))
            blkb.scalar(lambda e: b.run("scalar", e, S))
            blkb.vector(lambda e: b.run("vector", e, S))
            blkb.tensor(lambda e: b.run("tensor", e, S))
    if stages == 2:
        es.close()
        return nc

    b2 = Sched(base=b.cnt)
    B2W = [("GH2", b.cnt["GH2"]), ("G1", b.cnt["G1"]), ("SCD", PH1_SCD)]
    FQ_PER_EC = 2

    with ExitStack() as ph2:
        def sb2(name, shape, dt):
            return ph2.enter_context(nc.sbuf_tensor(name, shape, dt))

        hid = sb2("hid", [128, NEC * 512], BF16)
        h2T = sb2("h2T", [128, NDS * 512], BF16)
        h2rb = [sb2("h2rb0", [128, D], F32)]
        h2rb = [h2rb[0], h2rb[0]]
        wub = [sb2(f"wub{i}", [128, D], F32) for i in range(2)]
        wgb = [sb2(f"wgb{i}", [128, D], F32) for i in range(2)]
        tub = [sb2(f"tub{i}", [128, D], BF16) for i in range(2)]
        tgb = [sb2(f"tgb{i}", [128, D], BF16) for i in range(2)]
        ssb = sb2("ssb", [128, D], BF16)
        ssb2 = sb2("ssb2", [128, D], BF16)
        silb = [sb2("silb0", [128, 512], F32)]
        silb = [silb[0], silb[0]]
        wdb = [sb2(f"wdb{i}", [128, 512], F32) for i in range(2)]
        tdb = [sb2(f"tdb{i}", [128, 512], BF16) for i in range(2)]
        ffnT = sb2("ffnT", [128, NDS * 512], F32)
        x1rb = [sb2("x1rb0", [128, D], F32)]
        x1rb = [x1rb[0], x1rb[0]]
        t2b = [sb2("t2b0", [128, D], F32)]
        t2b = [t2b[0], t2b[0]]
        sq2 = ssb
        st2 = sb2("st2", [128, 16], F32)

        upg = ph2.enter_context(nc.psum_tensor("upg", [128, 2048], F32))
        dwp = ph2.enter_context(nc.psum_tensor("dwp", [128, 2048], F32))

        th2_hist = []
        for th in range(2):
            tuu_half0 = b2.cnt["TUU"]
            tug_half0 = b2.cnt["TUG"]
            ss_half0 = b2.cnt["SS"]
            vhid_half0 = b2.cnt["VHID"]
            td_half0 = b2.cnt["TD"]
            vf_half0 = b2.cnt["VF"]
            sf_half0 = b2.cnt["SF"]

            for i in ([] if "f2h" in KSKIP else range(4)):
                ip = i % 2
                w = list(B2W)
                if len(th2_hist) >= 1:
                    w.append(th2_hist[-1])
                b2.on("sync",
                      (lambda th_, i_, ip_: lambda e: e.dma_start(
                          out=h2rb[ip_][:, :],
                          in_=h2park[th_ * 512 + i_ * 128:
                                     th_ * 512 + (i_ + 1) * 128, :]))(th, i, ip),
                      waits=w, inc="H2R", inck=16)
                dq = b2.cnt["H2R"]
                for s4 in range(0, NDS, 4):
                    n4 = min(4, NDS - s4)
                    for k in range(n4):
                        s_ = s4 + k
                        w = [("H2R", dq), ("VH2", b2.cnt["VH2"])]
                        if th == 1:
                            w += [("SS", ss_half0), ("VHID", vhid_half0),
                                  ("VF", vf_half0), ("SF", sf_half0)]
                        b2.on("tensor",
                              (lambda ip_, s__, k_: lambda e: e.transpose(
                                  upg[:, k_ * 128:(k_ + 1) * 128],
                                  h2rb[ip_][:, s__ * 128:(s__ + 1) * 128],
                                  idf[:, :]))(ip, s_, k),
                              waits=w, inc="TH2" if k == n4 - 1 else None)
                    dst = h2T.ap().rearrange("p (s t) -> p s t", s=NDS)[
                        :, s4:s4 + n4, i * 128:(i + 1) * 128]
                    src = upg.ap().rearrange("p (k t) -> p k t", k=16)[
                        :, 0:n4, 0:128]
                    b2.on("vector",
                          (lambda d_, s_: lambda e: e.tensor_copy(d_, s_))(dst, src),
                          waits=[("TH2", b2.cnt["TH2"])], inc="VH2")
                th2_hist.append(("TH2", b2.cnt["TH2"]))
            vh2_half = b2.cnt["VH2"]

            fq_hist = []
            gtu_hist = []
            tuu_hist = []
            tug_hist = []
            ss_hist = []
            vhid_hist = []
            gtu_all = [("GTU", b2.cnt["GTU"]), ("GTU1", b2.cnt["GTU1"])]
            for ec in ([] if "f2w" in KSKIP else range(NEC)):
                ep = ec % 2
                wtfree = ([("TUU", tuu_hist[ec - 1][1]),
                           ("TUG", tug_hist[ec - 1][1])]
                          if ec >= 1 and len(tuu_hist) >= ec else [])
                if th == 0:
                    wfree = [fq_hist[ec - 2]] if (ec >= 2 and "f2q" not in KSKIP) else []
                    wg = list(B2W) + wfree
                    if ec >= 2:
                        wg.append(gtu_hist[ec - 2])
                    b2.on("gpsimd",
                          (lambda ec_, ep_: lambda e: e.dma_start(
                              out=wub[ep_][:, :], in_=wup_d[ec_, :, :]))(ec, ep),
                          waits=wg, inc="DQG", inck=16)
                    pp2 = "P0" if ec % 2 == 0 else "P1"
                    b2.on("sync",
                          (lambda ec_, ep_: lambda e: e.dma_start(
                              out=wgb[ep_][:, :], in_=wgt_d[ec_, :, :]))(ec, ep),
                          waits=list(wg), inc=pp2, inck=16)
                    dq = b2.cnt["DQG"]
                    dqg2 = (pp2, b2.cnt[pp2])
                    if "f2q" not in KSKIP and "f2qs" not in KSKIP:
                     b2.on("scalar",
                          (lambda ep_: lambda e: e.activation(
                              tub[ep_][:, :], wub[ep_][:, :], AF.Sign,
                              bias=scb[:, 4:5]))(ep),
                          waits=[("DQG", dq), ("SCD", PH1_SCD)] + wtfree)
                    if "f2q" not in KSKIP and "f2qs" not in KSKIP:
                     b2.on("scalar",
                          (lambda ep_: lambda e: e.activation(
                              ssb[:, :], wub[ep_][:, :], AF.Sign,
                              bias=scb[:, 3:4]))(ep), inc="FQS")
                     b2.on("vector",
                          (lambda ep_: lambda e: e.tensor_add(
                              tub[ep_][:, :], tub[ep_][:, :], ssb[:, :]))(ep),
                          waits=wtfree + [("FQS", b2.cnt["FQS"])], inc="FQ")
                    if "f2q" not in KSKIP and "f2qv" not in KSKIP:
                     b2.on("vector",
                          (lambda ep_: lambda e: e.tensor_single_scalar(
                              tgb[ep_][:, :], wgb[ep_][:, :], scb[:, 5:6],
                              ALU.is_gt))(ep),
                          waits=[dqg2, ("SCD", PH1_SCD)])
                     b2.on("vector",
                          (lambda ep_: lambda e: e.tensor_single_scalar(
                              ssb2[:, :], wgb[ep_][:, :], scb[:, 6:7],
                              ALU.is_lt))(ep))
                     b2.on("vector",
                          (lambda ep_: lambda e: e.tensor_sub(
                              tgb[ep_][:, :], tgb[ep_][:, :], ssb2[:, :]))(ep),
                          inc="FQ")
                    fq = b2.cnt["FQ"]
                    fq_hist.append(("FQ", fq))
                    gtup = "GTU" if ep == 0 else "GTU1"
                    if "f2p" not in KSKIP:
                        b2.on("gpsimd",
                              (lambda ec_, ep_: lambda e: e.dma_start(
                                  out=tqu_park[ec_, :, :], in_=tub[ep_][:, :]))(ec, ep),
                              waits=[("FQ", fq)], inc=gtup, inck=16)
                        b2.on("gpsimd",
                              (lambda ec_, ep_: lambda e: e.dma_start(
                                  out=tqg_park[ec_, :, :], in_=tgb[ep_][:, :]))(ec, ep),
                              inc=gtup, inck=16)
                    gtu_hist.append((gtup, b2.cnt[gtup]))
                    mmw = [("FQ", fq)]
                else:
                    if "f2p" in KSKIP:
                        continue
                    fwdp = "FWD" if ep == 0 else "FWD1"
                    wg = list(B2W) + wtfree + gtu_all
                    b2.on("gpsimd",
                          (lambda ec_, ep_: lambda e: e.dma_start(
                              out=tub[ep_][:, :], in_=tqu_park[ec_, :, :]))(ec, ep),
                          waits=wg, inc=fwdp, inck=16)
                    b2.on("gpsimd",
                          (lambda ec_, ep_: lambda e: e.dma_start(
                              out=tgb[ep_][:, :], in_=tqg_park[ec_, :, :]))(ec, ep),
                          inc=fwdp, inck=16)
                    mmw = [(fwdp, b2.cnt[fwdp])]
                for s_ in ([] if "f2u" in KSKIP else range(NDS)):
                    b2.on("tensor",
                          (lambda ep_, s__, f: lambda e: e.matmul(
                              upg[:, ep_ * 512:(ep_ + 1) * 512],
                              tub[ep_][:, s__ * 128:(s__ + 1) * 128],
                              h2T[:, s__ * 512:(s__ + 1) * 512],
                              start=(s__ == 0), stop=f,
                              skip_group_check=True))(ep, s_, s_ == NDS - 1),
                          waits=mmw + [("VH2", vh2_half)]
                          + ([("SS", ss_hist[ec - 1][1])]
                             if ec >= 1 and len(ss_hist) >= ec else []),
                          inc="TUU" if s_ == NDS - 1 else None)
                for s_ in ([] if "f2u" in KSKIP else range(NDS)):
                    b2.on("tensor",
                          (lambda ep_, s__, f: lambda e: e.matmul(
                              upg[:, 1024 + ep_ * 512:1024 + (ep_ + 1) * 512],
                              tgb[ep_][:, s__ * 128:(s__ + 1) * 128],
                              h2T[:, s__ * 512:(s__ + 1) * 512],
                              start=(s__ == 0), stop=f,
                              skip_group_check=True))(ep, s_, s_ == NDS - 1),
                          waits=([("VHID", vhid_hist[ec - 1][1])]
                                 if ec >= 1 and len(vhid_hist) >= ec else []),
                          inc="TUG" if s_ == NDS - 1 else None)
                if "f2u" in KSKIP:
                    continue
                b2.on("scalar",
                      (lambda ep_: lambda e: e.activation(
                          silb[ep_][:, :], upg[:, ep_ * 512:(ep_ + 1) * 512],
                          AF.Silu, scale=scb[:, 3:4]))(ep),
                      waits=[("TUU", b2.cnt["TUU"]), ("SCD", PH1_SCD)]
                      + ([("VHID", vhid_hist[ec - 1][1])]
                         if ec >= 1 and len(vhid_hist) >= ec else []),
                      inc="SS")
                hw = [("TUG", b2.cnt["TUG"]), ("SS", b2.cnt["SS"])]
                if th == 1:
                    hw.append(("TD", td_half0))
                b2.on("vector",
                      (lambda ec_, ep_: lambda e: e.scalar_tensor_tensor(
                          hid[:, ec_ * 512:(ec_ + 1) * 512],
                          upg[:, 1024 + ep_ * 512:1024 + (ep_ + 1) * 512],
                          1.0, silb[ep_][:, :], ALU.mult, ALU.mult))(ec, ep),
                      waits=hw, inc="VHID")
                tuu_hist.append(("TUU", b2.cnt["TUU"]))
                tug_hist.append(("TUG", b2.cnt["TUG"]))
                ss_hist.append(("SS", b2.cnt["SS"]))
                vhid_hist.append(("VHID", b2.cnt["VHID"]))
            vhid_half = b2.cnt["VHID"]

            if "f2d" in KSKIP:
                DGS_eff = []
            else:
                DGS_eff = DGS
            vft_base = b2.cnt["VFT"]
            gtd_all = [("GTD", b2.cnt["GTD"]), ("GTD1", b2.cnt["GTD1"])]
            fqd_hist = []
            gtd_hist = []
            didx = 0
            for gi, dg in enumerate(DGS_eff):
                alpha_col = 9 if gi % 2 == 0 else 10
                W = len(dg) * 128
                c0 = dg[0] * 128
                for ec in range(NEC):
                    ep = ec % 2
                    tdw = [("TD", b2.cnt["TD"] - 1)]
                    if th == 0:
                        wfree = [fqd_hist[didx - 2]] if didx >= 2 else []
                        wg = wfree + ([gtd_hist[didx - 2]] if didx >= 2 else [])
                        b2.on("gpsimd",
                              (lambda ec_, ep_, c_, w_: lambda e: e.dma_start(
                                  out=wdb[ep_][:, 0:w_],
                                  in_=wdn_d[ec_ * 128:(ec_ + 1) * 128,
                                            c_:c_ + w_]))(ec, ep, c0, W),
                              waits=wg, inc="DQG", inck=16)
                        dq = b2.cnt["DQG"]
                        if gi % 2 == 0:
                            b2.on("scalar",
                                  (lambda ep_, w_: lambda e: e.activation(
                                      tdb[ep_][:, 0:w_], wdb[ep_][:, 0:w_],
                                      AF.Sign, bias=scb[:, 8:9]))(ep, W),
                                  waits=[("DQG", dq)] + tdw)
                            b2.on("scalar",
                                  (lambda ep_, w_: lambda e: e.activation(
                                      ssb2[:, 0:w_], wdb[ep_][:, 0:w_], AF.Sign,
                                      bias=scb[:, 7:8]))(ep, W), inc="FQS")
                            b2.on("vector",
                                  (lambda ep_, w_: lambda e: e.tensor_add(
                                      tdb[ep_][:, 0:w_], tdb[ep_][:, 0:w_],
                                      ssb2[:, 0:w_]))(ep, W),
                                  waits=tdw + [("FQS", b2.cnt["FQS"])],
                                  inc="FQ")
                        else:
                            b2.on("vector",
                                  (lambda ep_, w_: lambda e: e.tensor_single_scalar(
                                      tdb[ep_][:, 0:w_], wdb[ep_][:, 0:w_],
                                      scb[:, 7:8], ALU.is_gt))(ep, W),
                                  waits=[("DQG", dq)] + tdw)
                            b2.on("vector",
                                  (lambda ep_, w_: lambda e: e.tensor_single_scalar(
                                      ssb2[:, 0:w_], wdb[ep_][:, 0:w_],
                                      scb[:, 8:9], ALU.is_lt))(ep, W))
                            b2.on("vector",
                                  (lambda ep_, w_: lambda e: e.tensor_sub(
                                      tdb[ep_][:, 0:w_], tdb[ep_][:, 0:w_],
                                      ssb2[:, 0:w_]))(ep, W),
                                  inc="FQ")
                        fqd_hist.append(("FQ", b2.cnt["FQ"]))
                        gtdp = "GTD" if didx % 2 == 0 else "GTD1"
                        b2.on("gpsimd",
                              (lambda gi_, ec_, ep_, w_: lambda e: e.dma_start(
                                  out=tqd_park[gi_, ec_, :, 0:w_],
                                  in_=tdb[ep_][:, 0:w_]))(gi, ec, ep, W),
                              waits=[("FQ", b2.cnt["FQ"])], inc=gtdp, inck=16)
                        gtd_hist.append((gtdp, b2.cnt[gtdp]))
                        mmw = [("FQ", b2.cnt["FQ"])]
                    else:
                        fddp = "FDD" if didx % 2 == 0 else "FDD1"
                        b2.on("gpsimd",
                              (lambda gi_, ec_, ep_, w_: lambda e: e.dma_start(
                                  out=tdb[ep_][:, 0:w_],
                                  in_=tqd_park[gi_, ec_, :, 0:w_]))(gi, ec, ep, W),
                              waits=tdw + gtd_all, inc=fddp, inck=16)
                        mmw = [(fddp, b2.cnt[fddp])]
                    didx += 1
                    dmw = mmw + [("VHID", vhid_half),
                                 ("VFT", vft_base + len(dg) * gi)]
                    if th == 1:
                        dmw += [("VF", vf_half0), ("SF", sf_half0)]
                    for k, dc in enumerate(dg):
                        b2.on("tensor",
                              (lambda ep_, k_, ec_: lambda e: e.matmul(
                                  dwp[:, k_ * 512:(k_ + 1) * 512],
                                  tdb[ep_][:, k_ * 128:(k_ + 1) * 128],
                                  hid[:, ec_ * 512:(ec_ + 1) * 512],
                                  start=(ec_ == 0), stop=(ec_ == NEC - 1),
                                  skip_group_check=True))(ep, k, ec),
                              waits=dmw,
                              inc="TD" if k == len(dg) - 1 else None)
                td = b2.cnt["TD"]
                for k, dc in enumerate(dg):
                    b2.on("vector",
                          (lambda k_, dc_, ac: lambda e: e.tensor_scalar(
                              ffnT[:, dc_ * 512:(dc_ + 1) * 512],
                              dwp[:, k_ * 512:(k_ + 1) * 512],
                              scb[:, ac:ac + 1], None, ALU.mult))(k, dc, alpha_col),
                          waits=[("TD", td), ("TF", b2.cnt["TF"])],
                          inc="VFT")

            vft_half = b2.cnt["VFT"]
            for i in ([] if "f2f" in KSKIP else range(4)):
                ip = i % 2
                fps = [dwp, upg][ip]
                for dc in range(NDS):
                    b2.on("tensor",
                          (lambda i_, dc_, f_: lambda e: e.transpose(
                              f_[:, dc_ * 128:(dc_ + 1) * 128],
                              ffnT[:, dc_ * 512 + i_ * 128:
                                   dc_ * 512 + (i_ + 1) * 128],
                              idf[:, :]))(i, dc, fps),
                          waits=[("VFT", vft_half),
                                 ("VF", b2.cnt["VF"] - 1),
                                 ("SF", b2.cnt["SF"] - 1)],
                          inc="TF" if dc == NDS - 1 else None)
                tf = b2.cnt["TF"]
                b2.on("sync",
                      (lambda th_, i_, ip_: lambda e: e.dma_start(
                          out=x1rb[ip_][:, :],
                          in_=x1park[th_ * 512 + i_ * 128:
                                     th_ * 512 + (i_ + 1) * 128, :]))(th, i, ip),
                      waits=[("VF", b2.cnt["VF"])],
                      inc="X1R", inck=16)
                dq = b2.cnt["X1R"]
                fv = fps[:, 0:NDS * 128]
                c2 = ip * 8
                if "f2fv" in KSKIP:
                    b2.on("vector",
                          (lambda ip_: lambda e: e.memset(t2b[ip_][:, :], 0.5))(ip),
                          waits=[("TF", tf)], inc="VF")
                if "f2fv" not in KSKIP:
                    b2.on("vector",
                      (lambda fv_, c_: lambda e: e.reduce_sum(
                          st2[:, c_:c_ + 1], fv_, AX.X))(fv, c2),
                      waits=[("TF", tf), ("SF", b2.cnt["SF"] - 1)], inc="VX")
                if "f2fv" not in KSKIP:
                    b2.on("vector",
                      (lambda c_: lambda e: e.tensor_scalar(
                          st2[:, c_ + 1:c_ + 2], st2[:, c_:c_ + 1],
                          -1.0 / D, None, ALU.mult))(c2),
                      waits=[("VX", b2.cnt["VX"])], inc="VX")
                    b2.on("vector",
                      (lambda c_: lambda e: e.tensor_scalar(
                          st2[:, c_ + 2:c_ + 3], st2[:, c_ + 1:c_ + 2],
                          st2[:, c_ + 1:c_ + 2], -1.0, ALU.mult,
                          ALU.mult))(c2),
                      waits=[("VX", b2.cnt["VX"])], inc="VS")
                vsv = b2.cnt["VS"]
                if "f2fs" not in KSKIP:
                    b2.on("scalar",
                      (lambda fv_, c_: lambda e: e.activation(
                          sq2[:, 0:NDS * 128], fv_, AF.Square,
                          accum_out=st2[:, c_ + 3:c_ + 4]))(fv, c2),
                      waits=[("TF", tf), ("VS", vsv)])
                if "f2fs" not in KSKIP:
                    b2.on("scalar",
                      (lambda c_: lambda e: e.activation(
                          st2[:, c_ + 3:c_ + 4], st2[:, c_ + 3:c_ + 4],
                          AF.Relu, bias=st2[:, c_ + 2:c_ + 3],
                          scale=1.0 / D))(c2),
                      waits=[("VS", vsv)])
                    b2.on("scalar",
                      (lambda c_: lambda e: e.activation(
                          st2[:, c_ + 3:c_ + 4], st2[:, c_ + 3:c_ + 4],
                          AF.Ln, bias=EPS_LN))(c2))
                    b2.on("scalar",
                      (lambda c_: lambda e: e.activation(
                          st2[:, c_ + 3:c_ + 4], st2[:, c_ + 3:c_ + 4],
                          AF.Exp, scale=-0.5))(c2),
                      inc="SF")
                sfv = b2.cnt["SF"]
                if "f2fv" not in KSKIP:
                    b2.on("vector",
                      (lambda ip_, fv_, c_: lambda e: e.tensor_scalar(
                          t2b[ip_][:, :], fv_, st2[:, c_ + 1:c_ + 2],
                          st2[:, c_ + 3:c_ + 4],
                          ALU.add, ALU.mult))(ip, fv, c2),
                      waits=[("SF", sfv), ("OUT", b2.cnt["OUT"])])
                    b2.on("vector",
                      (lambda ip_: lambda e: e.tensor_add(
                          t2b[ip_][:, :], t2b[ip_][:, :], x1rb[ip_][:, :]))(ip),
                      waits=[("X1R", dq)],
                      inc="VF")
                b2.on("gpsimd",
                      (lambda th_, i_, ip_: lambda e: e.dma_start(
                          out=out_d[th_ * 512 + i_ * 128:
                                    th_ * 512 + (i_ + 1) * 128, :],
                          in_=t2b[ip_][:, :]))(th, i, ip),
                      waits=[("VF", b2.cnt["VF"])], inc="OUT", inck=16)

        if KDBG:
            b2.on("gpsimd",
                  lambda e: e.dma_start(out=dbgq_d[:, 0:16], in_=scb[:, :]),
                  waits=[("SCD", PH1_SCD), ("OUT", b2.cnt["OUT"])],
                  inc="OUT", inck=16)
            b2.on("gpsimd",
                  lambda e: e.dma_start(out=dbgt_d[0, :, :],
                                        in_=tqd_park[1, 47, :, :]),
                  waits=[("GTD", b2.cnt["GTD"]), ("GTD1", b2.cnt["GTD1"])],
                  inc="OUT", inck=16)
            b2.on("gpsimd",
                  lambda e: e.dma_start(out=dbgt_d[1, :, :],
                                        in_=tqd_park[3, 34, :, :]),
                  inc="OUT", inck=16)
        b2.on("gpsimd", lambda e: e.nop(), waits=[("OUT", b2.cnt["OUT"])])

        with nc.Block() as blk2:
            blk2.sync(lambda e: b2.run("sync", e, S))
            blk2.gpsimd(lambda e: b2.run("gpsimd", e, S))
            blk2.scalar(lambda e: b2.run("scalar", e, S))
            blk2.vector(lambda e: b2.run("vector", e, S))
            blk2.tensor(lambda e: b2.run("tensor", e, S))

    es.close()
    return nc



def stage_core_inputs(x, c, nblk, H=16, NA=8, NB=16, qa=None, qb=None):
    D = H * 128
    bf = ml_dtypes.bfloat16
    if qa is None:
        qa, qb = c, nblk - 1 - c
    xq = np.concatenate([x[qa * BS:(qa + 1) * BS], x[qb * BS:(qb + 1) * BS]])
    ka = np.zeros((NA * BS, D), np.float32)
    kb = np.zeros((NB * BS, D), np.float32)
    areal = [qa] + list(range(qa))
    breal = [qb] + list(range(qb))
    for sl, blk in enumerate(areal):
        ka[sl * BS:(sl + 1) * BS] = x[blk * BS:(blk + 1) * BS]
    for sl, blk in enumerate(breal):
        kb[sl * BS:(sl + 1) * BS] = x[blk * BS:(blk + 1) * BS]
    wv = np.zeros((128, 4 * (NA + NB)), dtype=bf)
    wv[:, 0:4 * len(areal)] = 1
    wv[:, 4 * NA:4 * NA + 4 * len(breal)] = 1
    return {"xq": np.ascontiguousarray(xq), "ka": ka, "kb": kb, "wv": wv}


def stage_shared(W_up, W_gate, W_down, H=16):
    D = H * 128
    DE = W_up.shape[0]
    NEC = DE // 128
    bf = ml_dtypes.bfloat16
    wup_sw = np.ascontiguousarray(
        W_up.T.reshape(D // 128, 128, NEC, 128).transpose(2, 1, 0, 3)
        .reshape(NEC, 128, D))
    wgt_sw = np.ascontiguousarray(
        W_gate.T.reshape(D // 128, 128, NEC, 128).transpose(2, 1, 0, 3)
        .reshape(NEC, 128, D))
    wdn_T = np.ascontiguousarray(W_down.T)
    m = []
    for g in range(2):
        mg = np.zeros((128, 1024), dtype=bf)
        for cc in range(2):
            ch = 2 * g + cc
            kpos = ch * 128 + np.arange(128)[:, None]
            qpos = np.arange(512)[None, :]
            mg[:, cc * 512:(cc + 1) * 512] = (kpos <= qpos).astype(bf)
        m.append(mg)
    return {"wup": wup_sw, "wgt": wgt_sw, "wdn": wdn_T,
            "m0": m[0], "m1": m[1],
            "cstv": np.tile(np.array([[1e-5, 1e-8, 0.0, 0.0]], np.float32),
                            (128, 1)),
            "idb": np.eye(128, dtype=bf),
            "idf": np.eye(128, dtype=np.float32),
            "onef": np.ones((128, 1), dtype=np.float32)}


_NC_CACHE = {}


def kernel(L0_attn_x, L0_ffn_W_up, L0_ffn_W_gate, L0_ffn_W_down):
    x = np.ascontiguousarray(np.asarray(L0_attn_x)[0])
    L = x.shape[0]
    nblk = L // BS
    shared = stage_shared(np.asarray(L0_ffn_W_up), np.asarray(L0_ffn_W_gate),
                          np.asarray(L0_ffn_W_down))
    in_maps = [dict(stage_core_inputs(x, c, nblk), **shared)
               for c in range(NCORES)]
    if "full" not in _NC_CACHE:
        _NC_CACHE["full"] = build_program()
    res = run_bass_kernel_spmd(_NC_CACHE["full"], in_maps,
                               core_ids=list(range(NCORES)))
    out = np.empty((L, x.shape[1]), np.float32)
    for c in range(NCORES):
        o = res.results[c]["out"]
        qa, qb = c, nblk - 1 - c
        out[qa * BS:(qa + 1) * BS] = o[0:BS]
        out[qb * BS:(qb + 1) * BS] = o[BS:2 * BS]
    return out[None, :, :]
